# version 23
# baseline (speedup 1.0000x reference)
import contextlib
import os
import numpy as np
import ml_dtypes

import concourse.bass as bass
import concourse.mybir as mybir
from concourse.bass_utils import run_bass_kernel_spmd

F32 = mybir.dt.float32
BF16 = mybir.dt.bfloat16
AF = mybir.ActivationFunctionType
ALU = mybir.AluOpType
AX = mybir.AxisListType

SEQS = 2
T = 2048
D = 1024
H = 16
NTT = T // 128
FTOK = 1024
NFT = T // FTOK
NEG = -1e30


class Sem:
    def __init__(self, es, nc, name):
        self.h = es.enter_context(nc.semaphore(name))
        self.n = 0
        self.name = name


class Buf:
    def __init__(self, fence=None):
        self.w = dict(fence or {})
        self.r = {}

    def note_w(self, sem, n):
        if self.w.get(sem, 0) < n:
            self.w[sem] = n

    def note_r(self, sem, n):
        if self.r.get(sem, 0) < n:
            self.r[sem] = n


_PH = set(os.environ.get("KPHASES", "all").split(","))


def _on(p):
    return "all" in _PH or p in _PH


def build_nc():
    nc = bass.Bass("TRN2", target_bir_lowering=False)
    _nctr = [0]

    def uname(b):
        _nctr[0] += 1
        return f"{b}_{_nctr[0]}"

    x_d = nc.dram_tensor("x", [SEQS * T, D], F32, kind="ExternalInput").ap()
    y_d = nc.dram_tensor("y", [SEQS * T, D], F32, kind="ExternalOutput").ap()
    x1_d = nc.dram_tensor("x1scr", [SEQS * T, D], F32, kind="Internal").ap()

    qkvw_d = nc.dram_tensor("qkvw", [24, 128, 8, 128], F32, kind="ExternalInput").ap()
    outw_d = nc.dram_tensor("outw", [8, 128, 8, 128], F32, kind="ExternalInput").ap()
    w1_d = nc.dram_tensor("w1", [32, 128, 8, 128], F32, kind="ExternalInput").ap()
    w2_d = nc.dram_tensor("w2", [32, 128, 8, 128], F32, kind="ExternalInput").ap()
    h1_d = nc.dram_tensor("h1", [32, 128, 32, 128], F32, kind="ExternalInput").ap()
    h2_d = nc.dram_tensor("h2", [32, 128, 32, 128], F32, kind="ExternalInput").ap()
    w3_d = nc.dram_tensor("w3", [8, 128, 32, 128], F32, kind="ExternalInput").ap()
    s1_d = nc.dram_tensor("s1", [128, 32], F32, kind="ExternalInput").ap()
    s2_d = nc.dram_tensor("s2", [128, 32], F32, kind="ExternalInput").ap()
    sh1_d = nc.dram_tensor("sh1", [128, 32], F32, kind="ExternalInput").ap()
    sh2_d = nc.dram_tensor("sh2", [128, 32], F32, kind="ExternalInput").ap()
    s3_d = nc.dram_tensor("s3", [128, 8], F32, kind="ExternalInput").ap()
    hexw_d = nc.dram_tensor("hexw", [8, 128, 6], F32, kind="ExternalInput").ap()
    q6w_d = nc.dram_tensor("q6w", [8, 128, 6], F32, kind="ExternalInput").ap()
    tf_d = nc.dram_tensor("tf", [7, 36], F32, kind="ExternalInput").ap()
    tw_d = nc.dram_tensor("tw", [16, 7], F32, kind="ExternalInput").ap()
    lamlog_d = nc.dram_tensor("lamlog", [1, 1], F32, kind="ExternalInput").ap()
    slog_d = nc.dram_tensor("slog", [1, 1], F32, kind="ExternalInput").ap()
    alpha_d = nc.dram_tensor("alpha", [1, 1], F32, kind="ExternalInput").ap()
    beta_d = nc.dram_tensor("beta", [1, 1], F32, kind="ExternalInput").ap()
    identf_d = nc.dram_tensor("identf", [128, 128], F32, kind="ExternalInput").ap()
    identb_d = nc.dram_tensor("identb", [128, 128], BF16, kind="ExternalInput").ap()
    maskt_d = nc.dram_tensor("maskt", [128, 128], F32, kind="ExternalInput").ap()
    ones_d = nc.dram_tensor("ones", [1, 128], F32, kind="ExternalInput").ap()

    es = contextlib.ExitStack()

    _oneblk = os.environ.get("KONEBLK", "1") == "1"
    _cur = [None]

    _curcm = [None]

    def _open_group():
        if _curcm[0] is None:
            _curcm[0] = nc.Block()
            _cur[0] = _curcm[0].__enter__()

    def group_boundary():
        _flush()
        if os.environ.get("KBOUND", "0") == "0":
            return
        if _curcm[0] is not None:
            _curcm[0].__exit__(None, None, None)
            _curcm[0] = None
            _cur[0] = None

    class _GroupCloser:
        def __enter__(self):
            return self

        def __exit__(self, *a):
            group_boundary()
            return False

    es.enter_context(_GroupCloser())

    class _BlkReuse:
        def __enter__(self):
            _open_group()
            return _cur[0]

        def __exit__(self, *a):
            return False

    def _Block():
        if _oneblk:
            return _BlkReuse()
        return nc.Block()

    pe = Sem(es, nc, "s_pe")
    act = Sem(es, nc, "s_act")
    dve = Sem(es, nc, "s_dve")
    gps = Sem(es, nc, "s_gps")
    syn = Sem(es, nc, "s_syn")
    dw = Sem(es, nc, "s_dw")
    dx = Sem(es, nc, "s_dx")
    ALLSEMS = [pe, act, dve, gps, syn, dw, dx]
    ENG_SEM = {"tensor": pe, "scalar": act, "vector": dve, "gpsimd": gps, "sync": syn}
    _dma_sems = {}

    def slot_sem(key):
        sm = _dma_sems.get(key)
        if sm is None:
            sm = Sem(es, nc, "sd_" + key)
            _dma_sems[key] = sm
            ALLSEMS.append(sm)
        return sm

    def fence():
        return {s: s.n for s in ALLSEMS}

    _floors = {}
    _use_floors = os.environ.get("KFLOOR", "1") == "1"

    def _waits(e, own_sem, toks):
        fl = _floors.setdefault(own_sem, {})
        for s, v in toks.items():
            if s is own_sem or v <= 0:
                continue
            if _use_floors and fl.get(s, 0) >= v:
                continue
            e.wait_ge(s.h, v)
            fl[s] = v

    _pending = {"sync": [], "gpsimd": [], "tensor": [], "scalar": [], "vector": []}

    def _defer(eng, fn):
        _pending[eng].append(fn)

    def _flush():
        if not any(_pending.values()):
            return
        with _Block() as blk:
            for eng in ("sync", "gpsimd", "tensor", "scalar", "vector"):
                lst = _pending[eng]
                if lst:
                    def runner(e, lst=lst):
                        for fn in lst:
                            fn(e)
                    getattr(blk, eng)(runner)
                _pending[eng] = []

    class _EngProxy:
        def __init__(self, e):
            object.__setattr__(self, "_e", e)
            object.__setattr__(self, "last", None)

        def __getattr__(self, n):
            a = getattr(self._e, n)
            if callable(a):
                def call(*args, **kw):
                    r = a(*args, **kw)
                    if r is not None and hasattr(r, "then_inc"):
                        object.__setattr__(self, "last", r)
                    return r
                return call
            return a

    def sec(blk, eng, reads, writes, emit):
        own = ENG_SEM[eng]
        merged = {}
        for b in reads:
            for s, v in b.w.items():
                if merged.get(s, 0) < v:
                    merged[s] = v
        for b in writes:
            for toks in (b.w, b.r):
                for s, v in toks.items():
                    if merged.get(s, 0) < v:
                        merged[s] = v
        v = own.n + 1
        own.n = v
        for b in reads:
            b.note_r(own, v)
        for b in writes:
            b.note_w(own, v)

        def body(e):
            _waits(e, own, merged)
            p = _EngProxy(e)
            emit(p)
            if p.last is not None:
                p.last.then_inc(own.h, 1)
            else:
                e.nop().then_inc(own.h, 1)

        _defer(eng, body)

    def dma_load(blk, eng, dsem, out_ap, in_ap, dest_buf, key=None):
        own = ENG_SEM[eng]
        sm = slot_sem(key) if key is not None else dsem
        merged = {}
        for toks in (dest_buf.w, dest_buf.r):
            for s, v in toks.items():
                if merged.get(s, 0) < v:
                    merged[s] = v
        sm.n += 16
        dest_buf.note_w(sm, sm.n)

        def body(e):
            _waits(e, own, merged)
            e.dma_start(out_ap, in_ap).then_inc(sm.h, 16)

        _defer(eng, body)

    def dma_store(blk, eng, dsem, out_ap, in_ap, src_buf, key=None):
        own = ENG_SEM[eng]
        sm = slot_sem(key) if key is not None else dsem
        merged = dict(src_buf.w)
        sm.n += 16
        src_buf.note_r(sm, sm.n)

        def body(e):
            _waits(e, own, merged)
            e.dma_start(out_ap, in_ap).then_inc(sm.h, 16)

        _defer(eng, body)

    outw_bf = es.enter_context(nc.sbuf_tensor(uname("outw_bf"), [128, 8192], BF16))
    sc_sb = es.enter_context(nc.sbuf_tensor(uname("sc_sb"), [128, 136], F32))
    hexw_bf = es.enter_context(nc.sbuf_tensor(uname("hexw_bf"), [128, 48], BF16))
    q6w_bf = es.enter_context(nc.sbuf_tensor(uname("q6w_bf"), [128, 48], BF16))
    identf = es.enter_context(nc.sbuf_tensor(uname("identf_s"), [128, 128], F32))
    identb = es.enter_context(nc.sbuf_tensor(uname("identb_s"), [128, 128], BF16))
    maskt = es.enter_context(nc.sbuf_tensor(uname("maskt_s"), [128, 128], F32))
    maskb = es.enter_context(nc.sbuf_tensor(uname("maskb_s"), [128, 128], BF16))
    ones_sb = es.enter_context(nc.sbuf_tensor(uname("ones_sb"), [1, 128], F32))
    small = es.enter_context(nc.sbuf_tensor(uname("small"), [16, 96], F32))
    bc_sb = es.enter_context(nc.sbuf_tensor(uname("bc_sb"), [128, 8], F32))
    mh_sb = es.enter_context(nc.sbuf_tensor(uname("mh_sb"), [6, 96], BF16))
    wT_sb = es.enter_context(nc.sbuf_tensor(uname("wT_sb"), [7, 16], F32))
    tq_sb = es.enter_context(nc.sbuf_tensor(uname("tq_sb"), [12, T], BF16))
    tp_sb = es.enter_context(nc.sbuf_tensor(uname("tp_sb"), [12, T], BF16))
    tp2_sb = es.enter_context(nc.sbuf_tensor(uname("tp2_sb"), [12, T], BF16))
    hexc_sb = es.enter_context(nc.sbuf_tensor(uname("hexc_sb"), [6, T], BF16))
    c_eps = es.enter_context(nc.sbuf_tensor(uname("c_eps"), [128, 1], F32))
    c_two = es.enter_context(nc.sbuf_tensor(uname("c_two"), [128, 1], F32))
    c_nh = es.enter_context(nc.sbuf_tensor(uname("c_nh"), [128, 1], F32))
    nc.const_aps.register(F32, 1e-5, c_eps[:])
    nc.const_aps.register(F32, 2.0, c_two[:])
    nc.const_aps.register(F32, -0.5, c_nh[:])

    B_outw = Buf(); B_sc = Buf(); B_hexw = Buf(); B_q6w = Buf(); B_const = Buf()
    B_bc = Buf(); B_mh = Buf(); B_small = Buf(); B_wT = Buf()
    LAMH = bc_sb[0:6, 0:1]
    CSC = bc_sb[0:6, 1:2]
    IVBETA = bc_sb[:, 2:3]
    ALPHA = bc_sb[:, 3:4]

    with _Block() as blk:
        sec(blk, "gpsimd", [], [B_const], lambda e: (
            e.memset(c_eps[:], 1e-5), e.memset(c_two[:], 2.0), e.memset(c_nh[:], -0.5)))

        _pro = [
            (identf[:], identf_d), (identb[:], identb_d), (maskt[:], maskt_d),
            (ones_sb[:], ones_d),
            (sc_sb[:, 0:32], s1_d), (sc_sb[:, 32:64], s2_d),
            (sc_sb[:, 64:96], sh1_d), (sc_sb[:, 96:128], sh2_d),
            (sc_sb[:, 128:136], s3_d),
            (small[0:1, 0:1], lamlog_d), (small[0:1, 1:2], slog_d),
            (small[0:1, 2:3], alpha_d), (small[0:1, 3:4], beta_d),
            (small[0:16, 8:15], tw_d), (small[0:7, 40:76], tf_d),
        ]
        dw.n += 16 * len(_pro)
        B_const.note_w(dw, dw.n)
        B_sc.note_w(dw, dw.n)
        B_small.note_w(dw, dw.n)

        def pro_loads(e):
            for dst, src in _pro:
                e.dma_start(dst, src).then_inc(dw.h, 16)

        _defer("sync", pro_loads)

    with _Block() as blk, \
         nc.sbuf_tensor(uname("pstage"), [128, 1024], F32) as pstage, \
         nc.sbuf_tensor(uname("hstage"), [128, 96], F32) as hstage:
        B_pst = Buf(); B_hst = Buf()

        _smh = slot_sem("hst")
        _smh.n += 32
        B_hst.note_w(_smh, _smh.n)

        def hload(e, _smh=_smh):
            e.dma_start(hstage[:, 0:48].rearrange("p (k j) -> p k j", k=8),
                        hexw_d.rearrange("k p j -> p k j")).then_inc(_smh.h, 16)
            e.dma_start(hstage[:, 48:96].rearrange("p (k j) -> p k j", k=8),
                        q6w_d.rearrange("k p j -> p k j")).then_inc(_smh.h, 16)

        _defer("sync", hload)

        sec(blk, "vector", [B_hst, B_const], [B_hexw, B_q6w, B_const], lambda e: (
            e.tensor_copy(hexw_bf[:], hstage[:, 0:48]),
            e.tensor_copy(q6w_bf[:], hstage[:, 48:96]),
            e.tensor_copy(maskb[:], maskt[:])))

        for c in range(8):
            dma_load(blk, "sync", dw, pstage[:], outw_d[c], B_pst, key="pst")
            sec(blk, "vector", [B_pst], [B_outw],
                lambda e, c=c: e.tensor_copy(outw_bf[:, c * 1024:(c + 1) * 1024], pstage[:]))

        sec(blk, "scalar", [B_small], [B_small], lambda e: (
            e.activation(small[0:1, 4:5], small[0:1, 0:1], AF.Sigmoid, bias=0.0, scale=1.0),
            e.drain(),
            e.activation(small[0:1, 5:6], small[0:1, 1:2], AF.Sigmoid, bias=0.0, scale=1.0)))

        def derive(e):
            e.tensor_scalar(small[0:1, 80:81], small[0:1, 4:5], 0.5, None, ALU.mult)
            e.tensor_scalar(small[0:1, 81:82], small[0:1, 5:6], 1.0 / 3.0, None, ALU.mult)
            e.reciprocal(small[0:1, 82:83], small[0:1, 3:4])
            e.drain()
            e.tensor_copy(small[0:1, 83:84], small[0:1, 2:3])
            e.tensor_reduce(small[0:16, 16:17], small[0:16, 8:15], axis=AX.X, op=ALU.max)
            e.drain()
            e.tensor_scalar(small[0:16, 17:18], small[0:16, 16:17], -2.0, None, ALU.mult)
            e.drain()

        sec(blk, "vector", [B_small], [B_small], derive)

        sec(blk, "scalar", [B_small], [B_small], lambda e:
            e.activation(small[0:16, 24:31], small[0:16, 8:15], AF.Exp,
                         bias=small[0:16, 17:18], scale=2.0))

        def wnorm(e):
            e.reduce_sum(small[0:16, 18:19], small[0:16, 24:31], axis=AX.X)
            e.drain()
            e.reciprocal(small[0:16, 19:20], small[0:16, 18:19])
            e.drain()
            e.tensor_scalar(small[0:16, 32:39], small[0:16, 24:31], small[0:16, 19:20], None, ALU.mult)

        sec(blk, "vector", [B_small], [B_small], wnorm)

    with _Block() as blk, \
         nc.psum_tensor(uname("pb_ps"), [128, 8], F32) as pb_ps, \
         nc.psum_tensor(uname("pw_ps"), [7, 16], F32) as pw_ps, \
         nc.psum_tensor(uname("pm_ps"), [6, 16], F32) as pm_ps:
        B_pb = Buf(); B_pw = Buf(); B_pm = Buf()

        sec(blk, "tensor", [B_small, B_const], [B_pb, B_pw], lambda e: (
            e.matmul(pb_ps[:, 0:4], ones_sb[0:1, :], small[0:1, 80:84], start=True, stop=True),
            e.transpose(pw_ps[:], small[0:16, 32:39], identf[0:16, 0:16])))

        sec(blk, "vector", [B_pb, B_pw], [B_bc, B_wT], lambda e: (
            e.tensor_copy(bc_sb[:, 0:4], pb_ps[:, 0:4]),
            e.tensor_copy(wT_sb[:], pw_ps[:])))

        for ei in range(6):
            sec(blk, "tensor", [B_small, B_wT], [B_pm],
                lambda e, ei=ei: e.matmul(pm_ps[:], small[0:7, 40 + ei * 6:46 + ei * 6],
                                          wT_sb[:], start=True, stop=True))
            mh3 = mh_sb.rearrange("d (h e) -> d h e", e=6)
            sec(blk, "vector", [B_pm], [B_mh],
                lambda e, ei=ei: e.tensor_copy(mh3[:, :, ei], pm_ps[:]))

    def emit_ln(blk, xin_ap, mvt, B_x, B_mvt, out_ap, B_out, lnt_half=1024):
        def stats(e):
            e.bn_stats(mvt[:, 0:6], xin_ap[:, 0:512])
            e.bn_stats(mvt[:, 6:12], xin_ap[:, 512:1024])
            e.drain()
            e.bn_aggr(mvt[:, 12:14], mvt[:, 0:12])
            e.drain()

        sec(blk, "vector", [B_x], [B_mvt], stats)

        def rstd(e):
            e.activation(mvt[:, 14:15], mvt[:, 13:14], AF.Ln, bias=1e-5, scale=1.0)
            e.drain()
            e.activation(mvt[:, 15:16], mvt[:, 14:15], AF.Exp, bias=0.0, scale=-0.5)

        sec(blk, "scalar", [B_mvt], [B_mvt], rstd)

        def norm(e):
            e.scalar_tensor_tensor(mvt[:, 14:15], mvt[:, 12:13], -1.0, mvt[:, 15:16],
                                   ALU.mult, ALU.mult)
            e.drain()
            e.tensor_scalar(out_ap, xin_ap, mvt[:, 15:16], mvt[:, 14:15], ALU.mult, ALU.add)

        sec(blk, "vector", [B_x, B_mvt], [B_mvt, B_out], norm)

    def seq_attention(s):
        fn0 = fence()
        esq = contextlib.ExitStack()
        q_sb = esq.enter_context(nc.sbuf_tensor(uname("q_sb"), [128, 16384], BF16))
        k_sb = esq.enter_context(nc.sbuf_tensor(uname("k_sb"), [128, 16384], BF16))
        v_sb = esq.enter_context(nc.sbuf_tensor(uname("v_sb"), [128, 16384], BF16))
        B_q = Buf(fn0); B_k = Buf(fn0); B_v = Buf(fn0)
        B_tq = Buf(fn0); B_tp = Buf(fn0); B_tp2 = Buf(fn0); B_hexc = Buf(fn0)

        with esq:
            esa = contextlib.ExitStack()
            fn = fence()
            ln1f = esa.enter_context(nc.sbuf_tensor(uname("ln1f"), [128, 16384], BF16))
            xt = [esa.enter_context(nc.sbuf_tensor(uname(f"xt{i}"), [128, 1024], F32)) for i in range(2)]
            lnt = [esa.enter_context(nc.sbuf_tensor(uname(f"lnt{i}"), [128, 1024], BF16)) for i in range(2)]
            mvt = esa.enter_context(nc.sbuf_tensor(uname("mvt"), [128, 16], F32))
            pqf = [esa.enter_context(nc.sbuf_tensor(uname(f"pqf{i}"), [128, 1024], F32)) for i in range(2)]
            pqb = [esa.enter_context(nc.sbuf_tensor(uname(f"pqb{i}"), [128, 1024], BF16)) for i in range(2)]
            B_ln1f = [Buf(fn) for _ in range(4)]
            B_xt = [Buf(fn), Buf(fn)]
            B_lnt = [Buf(fn), Buf(fn)]
            B_mvt = Buf(fn)
            B_pqf = [Buf(fn), Buf(fn)]
            B_pqb = [Buf(fn), Buf(fn)]
            ln4 = ln1f.rearrange("p (t j tt) -> p t j tt", t=16, j=8)

            with esa:
              if _on("a1"):
                with _Block() as blk, \
                     nc.psum_tensor(uname("lnp0"), [128, 1024], BF16) as lnp0, \
                     nc.psum_tensor(uname("lnp1"), [128, 1024], BF16) as lnp1:
                    lnp = [lnp0, lnp1]
                    B_lnp = [Buf(fn), Buf(fn)]
                    ka1 = os.environ.get("KA1", "dlte")
                    for t in range(int(os.environ.get("KNTT", NTT))):
                        tb = t % 2
                        if "d" in ka1:
                            dma_load(blk, "gpsimd", dx, xt[tb][:],
                                     x_d[s * T + t * 128: s * T + (t + 1) * 128, :], B_xt[tb],
                                     key=f"xt{tb}")
                        if "l" in ka1:
                            emit_ln(blk, xt[tb][:], mvt, B_xt[tb], B_mvt, lnt[tb][:], B_lnt[tb])

                        def tps(e, tb=tb):
                            for j in range(8):
                                e.transpose(lnp[tb][:, j * 128:(j + 1) * 128],
                                            lnt[tb][:, j * 128:(j + 1) * 128], identb[:])

                        if "t" in ka1:
                            sec(blk, "tensor", [B_lnt[tb], B_const], [B_lnp[tb]], tps)
                        if "e" in ka1:
                            sec(blk, "vector", [B_lnp[tb]], [B_ln1f[t // 4]],
                                lambda e, t=t, tb=tb: e.tensor_copy(
                                    ln1f[:, t * 1024:(t + 1) * 1024], lnp[tb][:]))

              if _on("a2p"):
                with _Block() as blk, \
                     nc.psum_tensor(uname("bh_ps"), [6, 512], F32) as bh_ps:
                    B_bh = Buf(fence())
                    for which in range(2):
                        wbf = hexw_bf if which == 0 else q6w_bf
                        Bw = B_hexw if which == 0 else B_q6w
                        for tc in range(4):
                            def bmm(e, wbf=wbf, tc=tc):
                                for kc in range(8):
                                    e.matmul(bh_ps[:], wbf[:, kc * 6:kc * 6 + 6],
                                             ln4[:, tc * 4:(tc + 1) * 4, kc, :],
                                             start=(kc == 0), stop=(kc == 7))

                            sec(blk, "tensor", [B_ln1f[tc], Bw], [B_bh], bmm)
                            if which == 0:
                                sec(blk, "scalar", [B_bh], [B_tq],
                                    lambda e, tc=tc: e.activation(
                                        tq_sb[0:6, tc * 512:(tc + 1) * 512], bh_ps[:],
                                        AF.Tanh, bias=0.0, scale=1.0))
                            else:
                                sec(blk, "scalar", [B_bh], [B_hexc],
                                    lambda e, tc=tc: e.activation(
                                        hexc_sb[:, tc * 512:(tc + 1) * 512], bh_ps[:],
                                        AF.Tanh, bias=0.0, scale=1.0))

                    sec(blk, "vector", [B_bc, B_tq], [B_tp, B_tp2], lambda e: (
                        e.tensor_scalar(tp_sb[0:6, :], tq_sb[0:6, :], LAMH, None, ALU.mult),
                        e.tensor_scalar(tp2_sb[0:6, :], tq_sb[0:6, :], LAMH, None, ALU.mult)))
                    merged_tq = dict(B_hexc.w)
                    for tk in (B_tq.w, B_tq.r):
                        for ss, vv in tk.items():
                            if merged_tq.get(ss, 0) < vv:
                                merged_tq[ss] = vv
                    _smtq = slot_sem("tq")
                    _smtq.n += 16
                    B_tq.note_w(_smtq, _smtq.n)
                    B_hexc.note_r(_smtq, _smtq.n)

                    def tq_dma(e, merged_tq=merged_tq, _smtq=_smtq):
                        _waits(e, gps, merged_tq)
                        e.dma_start(tq_sb[6:12, :], hexc_sb[:]).then_inc(_smtq.h, 16)

                    _defer("gpsimd", tq_dma)

              if _on("a3"):
                with _Block() as blk, \
                     nc.psum_tensor(uname("qk_ps0"), [128, 512], F32) as qk_ps0, \
                     nc.psum_tensor(uname("qk_ps1"), [128, 512], F32) as qk_ps1:
                    qk_ps = [qk_ps0, qk_ps1]
                    fnq = fence()
                    B_qk = [Buf(fnq), Buf(fnq)]
                    for m in range(16):
                        tb = m % 2
                        dma_load(blk, "sync", dw, pqf[tb][:], qkvw_d[m], B_pqf[tb], key=f"pqf{tb}")
                        sec(blk, "vector", [B_pqf[tb]], [B_pqb[tb]],
                            lambda e, tb=tb: e.tensor_copy(pqb[tb][:], pqf[tb][:]))
                        dst = q_sb if m < 8 else k_sb
                        dstB = B_q if m < 8 else B_k
                        c = m % 8
                        for tc in range(4):
                            pp = tc % 2

                            def qmm(e, tb=tb, tc=tc, pp=pp):
                                for kc in range(8):
                                    e.matmul(qk_ps[pp][:], pqb[tb][:, kc * 128:(kc + 1) * 128],
                                             ln4[:, tc * 4:(tc + 1) * 4, kc, :],
                                             start=(kc == 0), stop=(kc == 7))

                            sec(blk, "tensor", [B_pqb[tb], B_ln1f[tc]], [B_qk[pp]], qmm)
                            sec(blk, "scalar", [B_qk[pp]], [dstB],
                                lambda e, pp=pp, dst=dst, c=c, tc=tc: e.activation(
                                    dst[:, c * 2048 + tc * 512: c * 2048 + (tc + 1) * 512],
                                    qk_ps[pp][:], AF.Copy, bias=0.0, scale=1.0))

                    for vm in range(8):
                        tb = vm % 2
                        dma_load(blk, "sync", dw, pqf[tb][:], qkvw_d[16 + vm], B_pqf[tb], key=f"pqf{tb}")
                        sec(blk, "vector", [B_pqf[tb]], [B_pqb[tb]],
                            lambda e, tb=tb: e.tensor_copy(pqb[tb][:], pqf[tb][:]))
                        for kt in range(NTT):
                            pp = kt % 2

                            def vmm(e, tb=tb, kt=kt, pp=pp):
                                for kc in range(8):
                                    e.matmul(qk_ps[pp][:, 0:128],
                                             ln4[:, kt, kc, :],
                                             pqb[tb][:, kc * 128:(kc + 1) * 128],
                                             start=(kc == 0), stop=(kc == 7))

                            sec(blk, "tensor", [B_pqb[tb], B_ln1f[kt // 4]], [B_qk[pp]], vmm)
                            eng = "scalar" if kt % 2 == 0 else "vector"
                            dstap = v_sb[:, kt * 1024 + vm * 128: kt * 1024 + (vm + 1) * 128]
                            if eng == "scalar":
                                sec(blk, eng, [B_qk[pp]], [B_v],
                                    lambda e, pp=pp, dstap=dstap: e.activation(
                                        dstap, qk_ps[pp][:, 0:128], AF.Copy, bias=0.0, scale=1.0))
                            else:
                                sec(blk, eng, [B_qk[pp]], [B_v],
                                    lambda e, pp=pp, dstap=dstap: e.tensor_copy(
                                        dstap, qk_ps[pp][:, 0:128]))

            esc = contextlib.ExitStack()
            fn = fence()
            attn_sb = esc.enter_context(nc.sbuf_tensor(uname("attn_sb"), [128, 16384], BF16))
            esc2 = contextlib.ExitStack()
            ah_tmp = esc2.enter_context(nc.sbuf_tensor(uname("ah_tmp"), [6, 2048], BF16))
            S_sb = [esc2.enter_context(nc.sbuf_tensor(uname(f"S_sb{i}"), [128, 2048], F32)) for i in range(2)]
            th_sb = [esc2.enter_context(nc.sbuf_tensor(uname(f"th{i}"), [128, 512], F32)) for i in range(2)]
            Pn_sb = [esc2.enter_context(nc.sbuf_tensor(uname(f"Pn_sb{i}"), [128, 2048], BF16)) for i in range(2)]
            PT_sb = [esc2.enter_context(nc.sbuf_tensor(uname(f"PT_sb{i}"), [128, 2048], BF16)) for i in range(2)]
            rs_sb = [esc2.enter_context(nc.sbuf_tensor(uname(f"rs_sb{i}"), [128, 8], F32)) for i in range(2)]
            B_attn = Buf(fn); B_ahtmp = Buf(fn)
            B_th = [Buf(fn), Buf(fn)]
            B_S = [Buf(fn), Buf(fn)]
            B_Pn = [Buf(fn), Buf(fn)]
            B_PT = [Buf(fn), Buf(fn)]
            B_rs = [Buf(fn), Buf(fn)]

            with esc:
              if _on("attn"):
                with _Block() as blk, \
                     nc.psum_tensor(uname("raw0"), [128, 512], F32) as raw0, \
                     nc.psum_tensor(uname("raw1"), [128, 512], F32) as raw1, \
                     nc.psum_tensor(uname("bb0"), [128, 512], F32) as bb0, \
                     nc.psum_tensor(uname("bb1"), [128, 512], F32) as bb1, \
                     nc.psum_tensor(uname("ptb0"), [128, 1024], BF16) as ptb0, \
                     nc.psum_tensor(uname("ptb1"), [128, 1024], BF16) as ptb1, \
                     nc.psum_tensor(uname("pv_ps0"), [128, 512], F32) as pv_ps0, \
                     nc.psum_tensor(uname("pv_ps1"), [128, 512], F32) as pv_ps1:
                    raw = [raw0, raw1]
                    bb = [bb0, bb1]
                    ptb = [ptb0, ptb1]
                    B_raw = [Buf(fn), Buf(fn)]
                    B_bb = [Buf(fn), Buf(fn)]
                    B_ptb = [Buf(fn), Buf(fn)]
                    pv_ps = [pv_ps0, pv_ps1]
                    B_pv = [Buf(fn), Buf(fn)]
                    ah_ps = raw0[0:6, :]
                    B_ah = B_raw[0]
                    mh3 = mh_sb.rearrange("d (h e) -> d h e", e=6)

                    pend = []

                    _pdepth = int(os.environ.get("KPD", "1"))

                    def flush_pend(all_=True):
                        while pend and (all_ or len(pend) >= _pdepth):
                            pend.pop(0)()

                    def ah_prep(h, tpt, B_tpt):
                        for tc in range(4):
                            sec(blk, "tensor", [B_hexc, B_mh], [B_ah],
                                lambda e, h=h, tc=tc: e.matmul(
                                    ah_ps[:], mh3[:, h, :],
                                    hexc_sb[:, tc * 512:(tc + 1) * 512], start=True, stop=True))
                            sec(blk, "vector", [B_ah, B_bc], [B_ahtmp],
                                lambda e, tc=tc: e.tensor_scalar(
                                    ah_tmp[:, tc * 512:(tc + 1) * 512], ah_ps[:],
                                    CSC, None, ALU.mult))

                        merged_tp = dict(B_ahtmp.w)
                        for tk in (B_tpt.w, B_tpt.r):
                            for ss, vv in tk.items():
                                if merged_tp.get(ss, 0) < vv:
                                    merged_tp[ss] = vv
                        _smtp = slot_sem("tp" if tpt is tp_sb else "tp2")
                        _smtp.n += 16
                        B_tpt.note_w(_smtp, _smtp.n)
                        B_ahtmp.note_r(_smtp, _smtp.n)

                        def tp_dma(e, merged_tp=merged_tp, tpt=tpt, _smtp=_smtp):
                            _waits(e, gps, merged_tp)
                            e.dma_start(tpt[6:12, :], ah_tmp[:]).then_inc(_smtp.h, 16)

                        _defer("gpsimd", tp_dma)

                    def chain(h, qi):
                            row = (qi + 1) * 128
                            nkc = (row + 511) // 512
                            qo = qi * 128
                            par = h % 2
                            hb = par * 64
                            hc = h // 2
                            tpt = tp_sb if par == 0 else tp2_sb
                            B_tpt = B_tp if par == 0 else B_tp2
                            pni = par
                            S_c = S_sb[par]; Pn_c = Pn_sb[pni]
                            PT_c = PT_sb[par]; rs_c = rs_sb[par]
                            B_S_c = B_S[par]; B_Pn_c = B_Pn[pni]
                            B_PT_c = B_PT[par]; B_rs_c = B_rs[par]

                            for kc in range(nkc):
                                klen = min(512, row - kc * 512)
                                pp = kc % 2

                                diag = (kc == nkc - 1)

                                def qkmm(e, h=h, kc=kc, klen=klen, pp=pp, hb=hb, hc=hc,
                                         qo=qo, diag=diag, tpt=tpt):
                                    e.matmul(raw[pp][:, 0:klen],
                                             q_sb[hb:hb + 64, hc * 2048 + qo: hc * 2048 + qo + 128],
                                             k_sb[hb:hb + 64, hc * 2048 + kc * 512: hc * 2048 + kc * 512 + klen],
                                             start=True, stop=True)
                                    e.matmul(bb[pp][:, 0:klen],
                                             tpt[:, qo:qo + 128],
                                             tq_sb[:, kc * 512:kc * 512 + klen],
                                             start=True, stop=not diag)
                                    if diag:
                                        e.matmul(bb[pp][:, klen - 128:klen], identb[:], maskb[:],
                                                 start=False, stop=True)

                                sec(blk, "tensor", [B_q, B_k, B_tpt, B_tq, B_const],
                                    [B_raw[pp], B_bb[pp]], qkmm)
                                sec(blk, "scalar", [B_raw[pp], B_bc], [B_th[pp]],
                                    lambda e, kc=kc, klen=klen, pp=pp: e.activation(
                                        th_sb[pp][:, 0:klen], raw[pp][:, 0:klen],
                                        AF.Tanh, bias=0.0, scale=IVBETA))

                                def sttm(e, kc=kc, klen=klen, pp=pp, S_c=S_c, rs_c=rs_c,
                                         last=(kc == nkc - 1), nkc=nkc):
                                    e.scalar_tensor_tensor(
                                        S_c[:, kc * 512:kc * 512 + klen],
                                        th_sb[pp][:, 0:klen], ALPHA, bb[pp][:, 0:klen],
                                        ALU.mult, ALU.add)
                                    e.drain()
                                    if nkc == 1:
                                        e.tensor_reduce(rs_c[:, 0:1], S_c[:, 0:klen],
                                                        axis=AX.X, op=ALU.max, negate=True)
                                        return
                                    e.tensor_reduce(rs_c[:, 4 + kc:5 + kc],
                                                    S_c[:, kc * 512:kc * 512 + klen],
                                                    axis=AX.X, op=ALU.max)
                                    if last:
                                        e.drain()
                                        e.tensor_reduce(rs_c[:, 0:1], rs_c[:, 4:4 + nkc],
                                                        axis=AX.X, op=ALU.max, negate=True)

                                sec(blk, "vector", [B_th[pp], B_bb[pp], B_bc],
                                    [B_S_c, B_rs_c], sttm)

                            sec(blk, "scalar", [B_S_c, B_rs_c], [B_Pn_c, B_rs_c],
                                lambda e, row=row, S_c=S_c, Pn_c=Pn_c, rs_c=rs_c: e.activation(
                                    Pn_c[:, 0:row], S_c[:, 0:row], AF.Exp,
                                    bias=rs_c[:, 0:1], scale=1.0, accum_out=rs_c[:, 1:2]))

                            def pnorm(e, row=row, Pn_c=Pn_c, rs_c=rs_c):
                                e.reciprocal(rs_c[:, 2:3], rs_c[:, 1:2])
                                e.drain()
                                e.tensor_scalar(Pn_c[:, 0:row], Pn_c[:, 0:row],
                                                rs_c[:, 2:3], None, ALU.mult)

                            sec(blk, "vector", [B_Pn_c, B_rs_c], [B_Pn_c, B_rs_c], pnorm)

                            def tail(h=h, qi=qi, hb=hb, hc=hc,
                                     Pn_c=Pn_c, PT_c=PT_c, B_Pn_c=B_Pn_c, B_PT_c=B_PT_c):
                                nbank = (qi + 8) // 8
                                for b in range(nbank):
                                    ktlo = b * 8
                                    kthi = min(qi + 1, ktlo + 8)

                                    def ptmm(e, b=b, ktlo=ktlo, kthi=kthi):
                                        for kt in range(ktlo, kthi):
                                            e.transpose(ptb[b][:, (kt - ktlo) * 128:(kt - ktlo + 1) * 128],
                                                        Pn_c[:, kt * 128:(kt + 1) * 128], identb[:])

                                    sec(blk, "tensor", [B_Pn_c, B_const], [B_ptb[b]], ptmm)
                                    sec(blk, "scalar", [B_ptb[b]], [B_PT_c],
                                        lambda e, b=b, ktlo=ktlo, kthi=kthi: e.activation(
                                            PT_c[:, ktlo * 128: kthi * 128],
                                            ptb[b][:, 0:(kthi - ktlo) * 128],
                                            AF.Copy, bias=0.0, scale=1.0))

                                pvp = pv_ps[h % 2]

                                def pvmm(e, pvp=pvp):
                                    for kt in range(qi + 1):
                                        e.matmul(pvp[0:64, 0:128],
                                                 v_sb[:, kt * 1024 + h * 64: kt * 1024 + (h + 1) * 64],
                                                 PT_c[:, kt * 128:(kt + 1) * 128],
                                                 start=(kt == 0), stop=(kt == qi))

                                sec(blk, "tensor", [B_PT_c, B_v], [B_pv[h % 2]], pvmm)

                                dstap = attn_sb[hb:hb + 64, hc * 2048 + qi * 128: hc * 2048 + (qi + 1) * 128]
                                if h % 2 == 0:
                                    sec(blk, "scalar", [B_pv[h % 2]], [B_attn],
                                        lambda e, pvp=pvp: e.activation(
                                            dstap, pvp[0:64, 0:128], AF.Copy, bias=0.0, scale=1.0))
                                else:
                                    sec(blk, "vector", [B_pv[h % 2]], [B_attn],
                                        lambda e, pvp=pvp: e.tensor_copy(dstap, pvp[0:64, 0:128]))

                            if os.environ.get("KPIPE", "1") == "1":
                                flush_pend(all_=False)
                                pend.append(tail)
                            else:
                                tail()

                    for hp in range(H // 2):
                        ah_prep(2 * hp, tp_sb, B_tp)
                        ah_prep(2 * hp + 1, tp2_sb, B_tp2)
                        for qi in range(NTT):
                            chain(2 * hp, qi)
                            chain(2 * hp + 1, qi)

                    flush_pend()

                esc2.close()

                if not _on("oproj"):
                    return
                eso = contextlib.ExitStack()
                fno = fence()
                xr = [eso.enter_context(nc.sbuf_tensor(uname(f"xr{i}"), [128, 1024], F32)) for i in range(2)]
                x1t = [eso.enter_context(nc.sbuf_tensor(uname(f"x1t{i}"), [128, 1024], F32)) for i in range(2)]
                B_xr = [Buf(fno), Buf(fno)]
                B_x1t = [Buf(fno), Buf(fno)]
                with eso:
                    with _Block() as blk, \
                         nc.psum_tensor(uname("op0"), [128, 512], F32) as op0, \
                         nc.psum_tensor(uname("op1"), [128, 512], F32) as op1:
                        op_ps = [op0, op1]
                        B_op = [Buf(fno), Buf(fno)]
                        for qi in range(NTT):
                            tb = qi % 2
                            dma_load(blk, "gpsimd", dx, xr[tb][:],
                                     x_d[s * T + qi * 128: s * T + (qi + 1) * 128, :], B_xr[tb],
                                     key=f"xr{tb}")
                            for nf in range(2):
                                def opmm(e, qi=qi, nf=nf):
                                    for c in range(8):
                                        e.matmul(op_ps[nf][:],
                                                 attn_sb[:, c * 2048 + qi * 128: c * 2048 + (qi + 1) * 128],
                                                 outw_bf[:, c * 1024 + nf * 512: c * 1024 + (nf + 1) * 512],
                                                 start=(c == 0), stop=(c == 7))

                                sec(blk, "tensor", [B_attn, B_outw], [B_op[nf]], opmm)
                                sec(blk, "vector", [B_op[nf], B_xr[tb]], [B_x1t[tb]],
                                    lambda e, qi=qi, nf=nf, tb=tb: e.tensor_tensor(
                                        x1t[tb][:, nf * 512:(nf + 1) * 512], op_ps[nf][:],
                                        xr[tb][:, nf * 512:(nf + 1) * 512], ALU.add))

                            dma_store(blk, "gpsimd", dx,
                                      x1_d[s * T + qi * 128: s * T + (qi + 1) * 128, :],
                                      x1t[tb][:], B_x1t[tb], key=f"x1t{tb}")

    def stream_panel4(blk, wd_m, phf, phb, B_phf, B_phb):
        src = wd_m.rearrange("p k j -> p (k j)")
        for qq in range(4):
            qb = qq % 2
            hh = qq // 2
            sl = qq % 2
            dma_load(blk, "sync", dw, phf[qb][:],
                     src[:, qq * 1024:(qq + 1) * 1024], B_phf[qb], key=f"phf{qb}")
            sec(blk, "vector", [B_phf[qb]], [B_phb[hh]],
                lambda e, qb=qb, hh=hh, sl=sl: e.tensor_copy(
                    phb[hh][:, sl * 1024:(sl + 1) * 1024], phf[qb][:]))

    def seq_ffn(s):
        for ft in range(NFT):
            base = s * T + ft * FTOK
            est = contextlib.ExitStack()
            fn = fence()
            h0 = est.enter_context(nc.sbuf_tensor(uname("h0"), [128, 32768], BF16))
            phf = [est.enter_context(nc.sbuf_tensor(uname(f"phf{i}"), [128, 1024], F32)) for i in range(2)]
            phb = [est.enter_context(nc.sbuf_tensor(uname(f"phb{i}"), [128, 2048], BF16)) for i in range(2)]
            B_h0 = Buf(fn)
            B_phf = [Buf(fn), Buf(fn)]
            B_phb = [Buf(fn), Buf(fn)]
            h03 = h0.rearrange("p (c t) -> p c t", c=32)

            with est:
                es1 = contextlib.ExitStack()
                ln2f = es1.enter_context(nc.sbuf_tensor(uname("ln2f"), [128, 8192], BF16))
                B_ln2f = [Buf(fn) for _ in range(2)]
                ln24 = ln2f.rearrange("p (t j tt) -> p t j tt", t=8, j=8)
                with es1:
                    es2 = contextlib.ExitStack()
                    fn2 = fence()
                    x1f = es2.enter_context(nc.sbuf_tensor(uname("x1f"), [128, 8192], F32))
                    lnt2 = [es2.enter_context(nc.sbuf_tensor(uname(f"lnt2{i}"), [128, 1024], BF16)) for i in range(2)]
                    mvt2 = es2.enter_context(nc.sbuf_tensor(uname("mvt2"), [128, 16], F32))
                    B_x1f = Buf(fn2)
                    B_lnt2 = [Buf(fn2), Buf(fn2)]
                    B_mvt2 = Buf(fn2)
                    with es2:
                      if _on("fln"):
                        with _Block() as blk, \
                             nc.psum_tensor(uname("lnq0"), [128, 1024], BF16) as lnq0, \
                             nc.psum_tensor(uname("lnq1"), [128, 1024], BF16) as lnq1:
                            lnq = [lnq0, lnq1]
                            B_lnq = [Buf(fn2), Buf(fn2)]
                            dma_load(blk, "gpsimd", dx,
                                     x1f[:].rearrange("p (a d) -> p a d", a=8),
                                     x1_d[base: base + FTOK, :].rearrange("(a p) d -> p a d", p=128),
                                     B_x1f, key="x1f")
                            for st in range(8):
                                tb = st % 2
                                emit_ln(blk, x1f[:, st * 1024:(st + 1) * 1024], mvt2,
                                        B_x1f, B_mvt2, lnt2[tb][:], B_lnt2[tb])

                                def tps2(e, tb=tb):
                                    for j in range(8):
                                        e.transpose(lnq[tb][:, j * 128:(j + 1) * 128],
                                                    lnt2[tb][:, j * 128:(j + 1) * 128], identb[:])

                                sec(blk, "tensor", [B_lnt2[tb], B_const], [B_lnq[tb]], tps2)
                                sec(blk, "vector", [B_lnq[tb]], [B_ln2f[st // 4]],
                                    lambda e, st=st, tb=tb: e.tensor_copy(
                                        ln2f[:, st * 1024:(st + 1) * 1024], lnq[tb][:]))

                    es3 = contextlib.ExitStack()
                    fn3 = fence()
                    pwf = [es3.enter_context(nc.sbuf_tensor(uname(f"pwf{i}"), [128, 1024], F32)) for i in range(2)]
                    pwb = [es3.enter_context(nc.sbuf_tensor(uname(f"pwb{i}"), [128, 1024], BF16)) for i in range(2)]
                    sil = [es3.enter_context(nc.sbuf_tensor(uname(f"sil{i}"), [128, 512], BF16)) for i in range(2)]
                    B_pwf = [Buf(fn3), Buf(fn3)]
                    B_pwb = [Buf(fn3), Buf(fn3)]
                    B_sil = [Buf(fn3), Buf(fn3)]
                    with es3:
                      if _on("fw12"):
                        with _Block() as blk, \
                             nc.psum_tensor(uname("wa0"), [128, 512], F32) as wa0, \
                             nc.psum_tensor(uname("wa1"), [128, 512], F32) as wa1, \
                             nc.psum_tensor(uname("wb0"), [128, 512], F32) as wb0, \
                             nc.psum_tensor(uname("wb1"), [128, 512], F32) as wb1:
                            wa = [wa0, wa1]
                            wb = [wb0, wb1]
                            B_wa = [Buf(fn3), Buf(fn3)]
                            B_wb = [Buf(fn3), Buf(fn3)]
                            for m in range(32):
                                dma_load(blk, "sync", dw, pwf[0][:], w1_d[m], B_pwf[0], key="pwf0")
                                sec(blk, "vector", [B_pwf[0]], [B_pwb[0]],
                                    lambda e: e.tensor_copy(pwb[0][:], pwf[0][:]))
                                dma_load(blk, "sync", dw, pwf[1][:], w2_d[m], B_pwf[1], key="pwf1")
                                sec(blk, "vector", [B_pwf[1]], [B_pwb[1]],
                                    lambda e: e.tensor_copy(pwb[1][:], pwf[1][:]))
                                for tc in range(2):
                                    def w1mm(e, tc=tc):
                                        for kc in range(8):
                                            e.matmul(wa[tc][:], pwb[0][:, kc * 128:(kc + 1) * 128],
                                                     ln24[:, tc * 4:(tc + 1) * 4, kc, :],
                                                     start=(kc == 0), stop=(kc == 7))

                                    def w2mm(e, tc=tc):
                                        for kc in range(8):
                                            e.matmul(wb[tc][:], pwb[1][:, kc * 128:(kc + 1) * 128],
                                                     ln24[:, tc * 4:(tc + 1) * 4, kc, :],
                                                     start=(kc == 0), stop=(kc == 7))

                                    sec(blk, "tensor", [B_pwb[0], B_ln2f[tc]], [B_wa[tc]], w1mm)
                                    sec(blk, "tensor", [B_pwb[1], B_ln2f[tc]], [B_wb[tc]], w2mm)
                                    sec(blk, "scalar", [B_wa[tc], B_sc], [B_sil[tc]],
                                        lambda e, m=m, tc=tc: e.activation(
                                            sil[tc][:], wa[tc][:], AF.Silu,
                                            bias=0.0, scale=sc_sb[:, m:m + 1]))
                                    sec(blk, "vector", [B_wb[tc], B_sil[tc], B_sc], [B_h0],
                                        lambda e, m=m, tc=tc: e.scalar_tensor_tensor(
                                            h0[:, m * 1024 + tc * 512: m * 1024 + (tc + 1) * 512],
                                            wb[tc][:], sc_sb[:, 32 + m:33 + m], sil[tc][:],
                                            ALU.mult, ALU.mult))

                es4 = contextlib.ExitStack()
                fn4 = fence()
                g1 = es4.enter_context(nc.sbuf_tensor(uname("g1"), [128, 32768], BF16))
                B_g1 = Buf(fn4)
                g13 = g1.rearrange("p (c t) -> p c t", c=32)
                with es4:
                    for stage in range(2 if _on("fh") else 0):
                        wd = h1_d if stage == 0 else h2_d
                        srcB = B_h0 if stage == 0 else B_g1
                        src3 = h03 if stage == 0 else g13
                        dst = g1 if stage == 0 else h0
                        dstB = B_g1 if stage == 0 else B_h0
                        scol = 64 if stage == 0 else 96
                        with _Block() as blk, \
                             nc.psum_tensor(uname("gp0"), [128, 512], F32) as gp0, \
                             nc.psum_tensor(uname("gp1"), [128, 512], F32) as gp1:
                            gp = [gp0, gp1]
                            fng = fence()
                            B_gp = [Buf(fng), Buf(fng)]
                            for m in range(32):
                                stream_panel4(blk, wd[m], phf, phb, B_phf, B_phb)
                                for tc in range(2):
                                    def hmm1(e, tc=tc, src3=src3):
                                        for kc in range(16):
                                            e.matmul(gp[tc][:], phb[0][:, kc * 128:(kc + 1) * 128],
                                                     src3[:, kc, tc * 512:(tc + 1) * 512],
                                                     start=(kc == 0), stop=False)

                                    def hmm2(e, tc=tc, src3=src3):
                                        for kc in range(16):
                                            e.matmul(gp[tc][:], phb[1][:, kc * 128:(kc + 1) * 128],
                                                     src3[:, 16 + kc, tc * 512:(tc + 1) * 512],
                                                     start=False, stop=(kc == 15))

                                    sec(blk, "tensor", [B_phb[0], srcB], [B_gp[tc]], hmm1)
                                    sec(blk, "tensor", [B_phb[1], srcB], [B_gp[tc]], hmm2)
                                    sec(blk, "scalar", [B_gp[tc], B_sc], [dstB],
                                        lambda e, m=m, tc=tc, dst=dst, scol=scol, gp=gp: e.activation(
                                            dst[:, m * 1024 + tc * 512: m * 1024 + (tc + 1) * 512],
                                            gp[tc][:], AF.Gelu, bias=0.0,
                                            scale=sc_sb[:, scol + m:scol + m + 1]))
                            _flush()

                es5 = contextlib.ExitStack()
                fn5 = fence()
                of = es5.enter_context(nc.sbuf_tensor(uname("of"), [128, 8192], F32))
                x1g = es5.enter_context(nc.sbuf_tensor(uname("x1g"), [128, 8192], F32))
                yt = [es5.enter_context(nc.sbuf_tensor(uname(f"yt{i}"), [128, 1024], F32)) for i in range(2)]
                ystg = es5.enter_context(nc.sbuf_tensor(uname("ystg"), [128, 1024], F32))
                B_of = Buf(fn5)
                B_x1g = Buf(fn5)
                B_yt = [Buf(fn5), Buf(fn5)]
                B_ystg = Buf(fn5)
                with es5:
                  if _on("fw3"):
                    with _Block() as blk, \
                         nc.psum_tensor(uname("o3p0"), [128, 512], F32) as o3p0, \
                         nc.psum_tensor(uname("o3p1"), [128, 512], F32) as o3p1:
                        o3p = [o3p0, o3p1]
                        B_o3 = [Buf(fn5), Buf(fn5)]
                        dma_load(blk, "gpsimd", dx,
                                 x1g[:].rearrange("p (a d) -> p a d", a=8),
                                 x1_d[base: base + FTOK, :].rearrange("(a p) d -> p a d", p=128),
                                 B_x1g, key="x1g")
                        for m in range(8):
                            stream_panel4(blk, w3_d[m], phf, phb, B_phf, B_phb)
                            for tc in range(2):
                                def w3mm1(e, tc=tc):
                                    for kc in range(16):
                                        e.matmul(o3p[tc][:], phb[0][:, kc * 128:(kc + 1) * 128],
                                                 h03[:, kc, tc * 512:(tc + 1) * 512],
                                                 start=(kc == 0), stop=False)

                                def w3mm2(e, tc=tc):
                                    for kc in range(16):
                                        e.matmul(o3p[tc][:], phb[1][:, kc * 128:(kc + 1) * 128],
                                                 h03[:, 16 + kc, tc * 512:(tc + 1) * 512],
                                                 start=False, stop=(kc == 15))

                                sec(blk, "tensor", [B_phb[0], B_h0], [B_o3[tc]], w3mm1)
                                sec(blk, "tensor", [B_phb[1], B_h0], [B_o3[tc]], w3mm2)
                                sec(blk, "vector", [B_o3[tc], B_sc], [B_of],
                                    lambda e, m=m, tc=tc: e.tensor_scalar(
                                        of[:, m * 1024 + tc * 512: m * 1024 + (tc + 1) * 512],
                                        o3p[tc][:], sc_sb[:, 128 + m:129 + m], None, ALU.mult))

                  if _on("fout"):
                    with _Block() as blk, \
                         nc.psum_tensor(uname("otp0"), [128, 512], F32) as otp0, \
                         nc.psum_tensor(uname("otp1"), [128, 512], F32) as otp1:
                        otp = [otp0, otp1]
                        fn6 = fence()
                        B_ot = [Buf(fn6), Buf(fn6)]
                        for st in range(8):
                            tb = st % 2
                            for bk in range(2):
                                def otmm(e, st=st, bk=bk):
                                    for cc in range(4):
                                        c = bk * 4 + cc
                                        e.transpose(otp[bk][:, cc * 128:(cc + 1) * 128],
                                                    of[:, c * 1024 + st * 128: c * 1024 + (st + 1) * 128],
                                                    identf[:])

                                sec(blk, "tensor", [B_of, B_const], [B_ot[bk]], otmm)
                                sec(blk, "vector", [B_ot[bk], B_x1g], [B_yt[tb]],
                                    lambda e, st=st, tb=tb, bk=bk: e.tensor_tensor(
                                        yt[tb][:, bk * 512:(bk + 1) * 512], otp[bk][:],
                                        x1g[:, st * 1024 + bk * 512: st * 1024 + bk * 512 + 512],
                                        ALU.add))

                            dma_store(blk, "gpsimd", dx,
                                      y_d[base + st * 128: base + (st + 1) * 128, :],
                                      yt[tb][:], B_yt[tb], key=f"yt{tb}")
                        _flush()

    nseq = int(os.environ.get("KSEQS", SEQS))
    with es:
        for s in range(nseq):
            if _on("seqattn"):
                seq_attention(s)
            group_boundary()
            if _on("seqffn"):
                seq_ffn(s)
            group_boundary()

        _fin = [(sm.h, sm.n) for sm in ALLSEMS[5:] if sm.n > 0]

        def final(e):
            for hh, vv in _fin:
                e.wait_ge(hh, vv)

        _defer("sync", final)
        group_boundary()

    return nc


def _stage_w(w):
    O, I = w.shape
    return np.ascontiguousarray(
        w.reshape(O // 128, 128, I // 128, 128).transpose(0, 3, 2, 1)).astype(np.float32)


def kernel(
    L0_attn_x, L0_attn_qkv_w, L0_attn_out_w, L0_attn_hex_w,
    L0_attn_hamming_lambda_logit, L0_attn_q6_w, L0_attn_transforms,
    L0_attn_transform_weights, L0_attn_scale_logit, L0_attn_sips_alpha,
    L0_attn_sips_beta, L0_ffn_w1_w, L0_ffn_w1_s, L0_ffn_w2_w, L0_ffn_w2_s,
    L0_ffn_h1_w, L0_ffn_h1_s, L0_ffn_h2_w, L0_ffn_h2_s, L0_ffn_w3_w, L0_ffn_w3_s,
):
    x = np.ascontiguousarray(np.asarray(L0_attn_x, dtype=np.float32))
    B = x.shape[0]
    n_cores = 8
    per = B // n_cores

    shared = {
        "qkvw": _stage_w(np.asarray(L0_attn_qkv_w, np.float32)),
        "outw": np.ascontiguousarray(
            np.asarray(L0_attn_out_w, np.float32).T.reshape(8, 128, 8, 128)),
        "w1": _stage_w(np.asarray(L0_ffn_w1_w, np.float32)),
        "w2": _stage_w(np.asarray(L0_ffn_w2_w, np.float32)),
        "h1": _stage_w(np.asarray(L0_ffn_h1_w, np.float32)),
        "h2": _stage_w(np.asarray(L0_ffn_h2_w, np.float32)),
        "w3": _stage_w(np.asarray(L0_ffn_w3_w, np.float32)),
        "s1": np.ascontiguousarray(np.asarray(L0_ffn_w1_s, np.float32).reshape(32, 128).T),
        "s2": np.ascontiguousarray(np.asarray(L0_ffn_w2_s, np.float32).reshape(32, 128).T),
        "sh1": np.ascontiguousarray(np.asarray(L0_ffn_h1_s, np.float32).reshape(32, 128).T),
        "sh2": np.ascontiguousarray(np.asarray(L0_ffn_h2_s, np.float32).reshape(32, 128).T),
        "s3": np.ascontiguousarray(np.asarray(L0_ffn_w3_s, np.float32).reshape(8, 128).T),
        "hexw": np.ascontiguousarray(np.asarray(L0_attn_hex_w, np.float32).T.reshape(8, 128, 6)),
        "q6w": np.ascontiguousarray(np.asarray(L0_attn_q6_w, np.float32).T.reshape(8, 128, 6)),
        "tf": np.ascontiguousarray(np.asarray(L0_attn_transforms, np.float32).transpose(0, 2, 1).reshape(7, 36)),
        "tw": np.ascontiguousarray(np.asarray(L0_attn_transform_weights, np.float32)),
        "lamlog": np.asarray(L0_attn_hamming_lambda_logit, np.float32).reshape(1, 1),
        "slog": np.asarray(L0_attn_scale_logit, np.float32).reshape(1, 1),
        "alpha": np.asarray(L0_attn_sips_alpha, np.float32).reshape(1, 1),
        "beta": np.asarray(L0_attn_sips_beta, np.float32).reshape(1, 1),
        "identf": np.eye(128, dtype=np.float32),
        "identb": np.eye(128).astype(ml_dtypes.bfloat16),
        "maskt": np.where(np.arange(128)[None, :] > np.arange(128)[:, None],
                          np.float32(NEG), np.float32(0.0)).astype(np.float32),
        "ones": np.ones((1, 128), np.float32),
    }

    nc = build_nc()
    in_maps = []
    for i in range(n_cores):
        m = dict(shared)
        m["x"] = np.ascontiguousarray(x[i * per:(i + 1) * per].reshape(per * T, D))
        in_maps.append(m)

    res = run_bass_kernel_spmd(nc, in_maps, core_ids=list(range(n_cores)))
    out = np.concatenate(
        [np.asarray(res.results[i]["y"]).reshape(per, T, D) for i in range(n_cores)], axis=0)
    return out.astype(np.float32)


# version 24 (final)
# speedup vs baseline: 1.0783x; 1.0783x over previous
import contextlib
import os
import numpy as np
import ml_dtypes

import concourse.bass as bass
import concourse.mybir as mybir
from concourse.bass_utils import run_bass_kernel_spmd

F32 = mybir.dt.float32
BF16 = mybir.dt.bfloat16
AF = mybir.ActivationFunctionType
ALU = mybir.AluOpType
AX = mybir.AxisListType

SEQS = 2
T = 2048
D = 1024
H = 16
NTT = T // 128
FTOK = 1024
NFT = T // FTOK
NEG = -1e30


class Sem:
    def __init__(self, es, nc, name):
        self.h = es.enter_context(nc.semaphore(name))
        self.n = 0
        self.name = name


class Buf:
    def __init__(self, fence=None):
        self.w = dict(fence or {})
        self.r = {}

    def note_w(self, sem, n):
        if self.w.get(sem, 0) < n:
            self.w[sem] = n

    def note_r(self, sem, n):
        if self.r.get(sem, 0) < n:
            self.r[sem] = n


_PH = set(os.environ.get("KPHASES", "all").split(","))


def _on(p):
    return "all" in _PH or p in _PH


def build_nc():
    nc = bass.Bass("TRN2", target_bir_lowering=False)
    _nctr = [0]

    def uname(b):
        _nctr[0] += 1
        return f"{b}_{_nctr[0]}"

    x_d = nc.dram_tensor("x", [SEQS * T, D], F32, kind="ExternalInput").ap()
    y_d = nc.dram_tensor("y", [SEQS * T, D], F32, kind="ExternalOutput").ap()
    x1_d = nc.dram_tensor("x1scr", [SEQS * T, D], F32, kind="Internal").ap()

    qkvw_d = nc.dram_tensor("qkvw", [24, 128, 8, 128], F32, kind="ExternalInput").ap()
    outw_d = nc.dram_tensor("outw", [8, 128, 8, 128], F32, kind="ExternalInput").ap()
    w1_d = nc.dram_tensor("w1", [32, 128, 8, 128], F32, kind="ExternalInput").ap()
    w2_d = nc.dram_tensor("w2", [32, 128, 8, 128], F32, kind="ExternalInput").ap()
    h1_d = nc.dram_tensor("h1", [32, 128, 32, 128], F32, kind="ExternalInput").ap()
    h2_d = nc.dram_tensor("h2", [32, 128, 32, 128], F32, kind="ExternalInput").ap()
    w3_d = nc.dram_tensor("w3", [8, 128, 32, 128], F32, kind="ExternalInput").ap()
    s1_d = nc.dram_tensor("s1", [128, 32], F32, kind="ExternalInput").ap()
    s2_d = nc.dram_tensor("s2", [128, 32], F32, kind="ExternalInput").ap()
    sh1_d = nc.dram_tensor("sh1", [128, 32], F32, kind="ExternalInput").ap()
    sh2_d = nc.dram_tensor("sh2", [128, 32], F32, kind="ExternalInput").ap()
    s3_d = nc.dram_tensor("s3", [128, 8], F32, kind="ExternalInput").ap()
    hexw_d = nc.dram_tensor("hexw", [8, 128, 6], F32, kind="ExternalInput").ap()
    q6w_d = nc.dram_tensor("q6w", [8, 128, 6], F32, kind="ExternalInput").ap()
    tf_d = nc.dram_tensor("tf", [7, 36], F32, kind="ExternalInput").ap()
    tw_d = nc.dram_tensor("tw", [16, 7], F32, kind="ExternalInput").ap()
    lamlog_d = nc.dram_tensor("lamlog", [1, 1], F32, kind="ExternalInput").ap()
    slog_d = nc.dram_tensor("slog", [1, 1], F32, kind="ExternalInput").ap()
    alpha_d = nc.dram_tensor("alpha", [1, 1], F32, kind="ExternalInput").ap()
    beta_d = nc.dram_tensor("beta", [1, 1], F32, kind="ExternalInput").ap()
    identf_d = nc.dram_tensor("identf", [128, 128], F32, kind="ExternalInput").ap()
    identb_d = nc.dram_tensor("identb", [128, 128], BF16, kind="ExternalInput").ap()
    maskt_d = nc.dram_tensor("maskt", [128, 128], F32, kind="ExternalInput").ap()
    ones_d = nc.dram_tensor("ones", [1, 128], F32, kind="ExternalInput").ap()

    es = contextlib.ExitStack()

    _oneblk = os.environ.get("KONEBLK", "1") == "1"
    _cur = [None]

    _curcm = [None]

    def _open_group():
        if _curcm[0] is None:
            _curcm[0] = nc.Block()
            _cur[0] = _curcm[0].__enter__()

    def group_boundary():
        _flush()
        if os.environ.get("KBOUND", "0") == "0":
            return
        if _curcm[0] is not None:
            _curcm[0].__exit__(None, None, None)
            _curcm[0] = None
            _cur[0] = None

    class _GroupCloser:
        def __enter__(self):
            return self

        def __exit__(self, *a):
            group_boundary()
            return False

    es.enter_context(_GroupCloser())

    class _BlkReuse:
        def __enter__(self):
            _open_group()
            return _cur[0]

        def __exit__(self, *a):
            return False

    def _Block():
        if _oneblk:
            return _BlkReuse()
        return nc.Block()

    pe = Sem(es, nc, "s_pe")
    act = Sem(es, nc, "s_act")
    dve = Sem(es, nc, "s_dve")
    gps = Sem(es, nc, "s_gps")
    syn = Sem(es, nc, "s_syn")
    dw = Sem(es, nc, "s_dw")
    dx = Sem(es, nc, "s_dx")
    ALLSEMS = [pe, act, dve, gps, syn, dw, dx]
    ENG_SEM = {"tensor": pe, "scalar": act, "vector": dve, "gpsimd": gps, "sync": syn}
    _dma_sems = {}

    def slot_sem(key):
        sm = _dma_sems.get(key)
        if sm is None:
            sm = Sem(es, nc, "sd_" + key)
            _dma_sems[key] = sm
            ALLSEMS.append(sm)
        return sm

    def fence():
        return {s: s.n for s in ALLSEMS}

    _floors = {}
    _use_floors = os.environ.get("KFLOOR", "1") == "1"

    def _waits(e, own_sem, toks):
        fl = _floors.setdefault(own_sem, {})
        for s, v in toks.items():
            if s is own_sem or v <= 0:
                continue
            if _use_floors and fl.get(s, 0) >= v:
                continue
            e.wait_ge(s.h, v)
            fl[s] = v

    _pending = {"sync": [], "gpsimd": [], "tensor": [], "scalar": [], "vector": []}

    def _defer(eng, fn):
        _pending[eng].append(fn)

    def _flush():
        if not any(_pending.values()):
            return
        with _Block() as blk:
            for eng in ("sync", "gpsimd", "tensor", "scalar", "vector"):
                lst = _pending[eng]
                if lst:
                    def runner(e, lst=lst):
                        for fn in lst:
                            fn(e)
                    getattr(blk, eng)(runner)
                _pending[eng] = []

    class _EngProxy:
        def __init__(self, e):
            object.__setattr__(self, "_e", e)
            object.__setattr__(self, "last", None)

        def __getattr__(self, n):
            a = getattr(self._e, n)
            if callable(a):
                def call(*args, **kw):
                    r = a(*args, **kw)
                    if r is not None and hasattr(r, "then_inc"):
                        object.__setattr__(self, "last", r)
                    return r
                return call
            return a

    def sec(blk, eng, reads, writes, emit):
        own = ENG_SEM[eng]
        merged = {}
        for b in reads:
            for s, v in b.w.items():
                if merged.get(s, 0) < v:
                    merged[s] = v
        for b in writes:
            for toks in (b.w, b.r):
                for s, v in toks.items():
                    if merged.get(s, 0) < v:
                        merged[s] = v
        v = own.n + 1
        own.n = v
        for b in reads:
            b.note_r(own, v)
        for b in writes:
            b.note_w(own, v)

        def body(e):
            _waits(e, own, merged)
            p = _EngProxy(e)
            emit(p)
            if p.last is not None:
                p.last.then_inc(own.h, 1)
            else:
                e.nop().then_inc(own.h, 1)

        _defer(eng, body)

    def dma_load(blk, eng, dsem, out_ap, in_ap, dest_buf, key=None):
        own = ENG_SEM[eng]
        sm = slot_sem(key) if key is not None else dsem
        merged = {}
        for toks in (dest_buf.w, dest_buf.r):
            for s, v in toks.items():
                if merged.get(s, 0) < v:
                    merged[s] = v
        sm.n += 16
        dest_buf.note_w(sm, sm.n)

        def body(e):
            _waits(e, own, merged)
            e.dma_start(out_ap, in_ap).then_inc(sm.h, 16)

        _defer(eng, body)

    def dma_store(blk, eng, dsem, out_ap, in_ap, src_buf, key=None):
        own = ENG_SEM[eng]
        sm = slot_sem(key) if key is not None else dsem
        merged = dict(src_buf.w)
        sm.n += 16
        src_buf.note_r(sm, sm.n)

        def body(e):
            _waits(e, own, merged)
            e.dma_start(out_ap, in_ap).then_inc(sm.h, 16)

        _defer(eng, body)

    outw_bf = es.enter_context(nc.sbuf_tensor(uname("outw_bf"), [128, 8192], BF16))
    sc_sb = es.enter_context(nc.sbuf_tensor(uname("sc_sb"), [128, 136], F32))
    hexw_bf = es.enter_context(nc.sbuf_tensor(uname("hexw_bf"), [128, 48], BF16))
    q6w_bf = es.enter_context(nc.sbuf_tensor(uname("q6w_bf"), [128, 48], BF16))
    identf = es.enter_context(nc.sbuf_tensor(uname("identf_s"), [128, 128], F32))
    identb = es.enter_context(nc.sbuf_tensor(uname("identb_s"), [128, 128], BF16))
    maskt = es.enter_context(nc.sbuf_tensor(uname("maskt_s"), [128, 128], F32))
    maskb = es.enter_context(nc.sbuf_tensor(uname("maskb_s"), [128, 128], BF16))
    ones_sb = es.enter_context(nc.sbuf_tensor(uname("ones_sb"), [1, 128], F32))
    small = es.enter_context(nc.sbuf_tensor(uname("small"), [16, 96], F32))
    bc_sb = es.enter_context(nc.sbuf_tensor(uname("bc_sb"), [128, 8], F32))
    mh_sb = es.enter_context(nc.sbuf_tensor(uname("mh_sb"), [6, 96], BF16))
    wT_sb = es.enter_context(nc.sbuf_tensor(uname("wT_sb"), [7, 16], F32))
    tq_sb = es.enter_context(nc.sbuf_tensor(uname("tq_sb"), [12, T], BF16))
    tp_sb = es.enter_context(nc.sbuf_tensor(uname("tp_sb"), [12, T], BF16))
    tp2_sb = es.enter_context(nc.sbuf_tensor(uname("tp2_sb"), [12, T], BF16))
    hexc_sb = es.enter_context(nc.sbuf_tensor(uname("hexc_sb"), [6, T], BF16))
    c_eps = es.enter_context(nc.sbuf_tensor(uname("c_eps"), [128, 1], F32))
    c_two = es.enter_context(nc.sbuf_tensor(uname("c_two"), [128, 1], F32))
    c_nh = es.enter_context(nc.sbuf_tensor(uname("c_nh"), [128, 1], F32))
    nc.const_aps.register(F32, 1e-5, c_eps[:])
    nc.const_aps.register(F32, 2.0, c_two[:])
    nc.const_aps.register(F32, -0.5, c_nh[:])

    B_outw = Buf(); B_sc = Buf(); B_hexw = Buf(); B_q6w = Buf(); B_const = Buf()
    B_bc = Buf(); B_mh = Buf(); B_small = Buf(); B_wT = Buf()
    LAMH = bc_sb[0:6, 0:1]
    CSC = bc_sb[0:6, 1:2]
    IVBETA = bc_sb[:, 2:3]
    ALPHA = bc_sb[:, 3:4]

    with _Block() as blk:
        sec(blk, "gpsimd", [], [B_const], lambda e: (
            e.memset(c_eps[:], 1e-5), e.memset(c_two[:], 2.0), e.memset(c_nh[:], -0.5)))

        _pro = [
            (identf[:], identf_d), (identb[:], identb_d), (maskt[:], maskt_d),
            (ones_sb[:], ones_d),
            (sc_sb[:, 0:32], s1_d), (sc_sb[:, 32:64], s2_d),
            (sc_sb[:, 64:96], sh1_d), (sc_sb[:, 96:128], sh2_d),
            (sc_sb[:, 128:136], s3_d),
            (small[0:1, 0:1], lamlog_d), (small[0:1, 1:2], slog_d),
            (small[0:1, 2:3], alpha_d), (small[0:1, 3:4], beta_d),
            (small[0:16, 8:15], tw_d), (small[0:7, 40:76], tf_d),
        ]
        dw.n += 16 * len(_pro)
        B_const.note_w(dw, dw.n)
        B_sc.note_w(dw, dw.n)
        B_small.note_w(dw, dw.n)

        def pro_loads(e):
            for dst, src in _pro:
                e.dma_start(dst, src).then_inc(dw.h, 16)

        _defer("sync", pro_loads)

    with _Block() as blk, \
         nc.sbuf_tensor(uname("pstage"), [128, 1024], F32) as pstage, \
         nc.sbuf_tensor(uname("hstage"), [128, 96], F32) as hstage:
        B_pst = Buf(); B_hst = Buf()

        _smh = slot_sem("hst")
        _smh.n += 32
        B_hst.note_w(_smh, _smh.n)

        def hload(e, _smh=_smh):
            e.dma_start(hstage[:, 0:48].rearrange("p (k j) -> p k j", k=8),
                        hexw_d.rearrange("k p j -> p k j")).then_inc(_smh.h, 16)
            e.dma_start(hstage[:, 48:96].rearrange("p (k j) -> p k j", k=8),
                        q6w_d.rearrange("k p j -> p k j")).then_inc(_smh.h, 16)

        _defer("sync", hload)

        sec(blk, "vector", [B_hst, B_const], [B_hexw, B_q6w, B_const], lambda e: (
            e.tensor_copy(hexw_bf[:], hstage[:, 0:48]),
            e.tensor_copy(q6w_bf[:], hstage[:, 48:96]),
            e.tensor_copy(maskb[:], maskt[:])))

        for c in range(8):
            dma_load(blk, "sync", dw, pstage[:], outw_d[c], B_pst, key="pst")
            sec(blk, "vector", [B_pst], [B_outw],
                lambda e, c=c: e.tensor_copy(outw_bf[:, c * 1024:(c + 1) * 1024], pstage[:]))

        sec(blk, "scalar", [B_small], [B_small], lambda e: (
            e.activation(small[0:1, 4:5], small[0:1, 0:1], AF.Sigmoid, bias=0.0, scale=1.0),
            e.drain(),
            e.activation(small[0:1, 5:6], small[0:1, 1:2], AF.Sigmoid, bias=0.0, scale=1.0)))

        def derive(e):
            e.tensor_scalar(small[0:1, 80:81], small[0:1, 4:5], 0.5, None, ALU.mult)
            e.tensor_scalar(small[0:1, 81:82], small[0:1, 5:6], 1.0 / 3.0, None, ALU.mult)
            e.reciprocal(small[0:1, 82:83], small[0:1, 3:4])
            e.drain()
            e.tensor_copy(small[0:1, 83:84], small[0:1, 2:3])
            e.tensor_reduce(small[0:16, 16:17], small[0:16, 8:15], axis=AX.X, op=ALU.max)
            e.drain()
            e.tensor_scalar(small[0:16, 17:18], small[0:16, 16:17], -2.0, None, ALU.mult)
            e.drain()

        sec(blk, "vector", [B_small], [B_small], derive)

        sec(blk, "scalar", [B_small], [B_small], lambda e:
            e.activation(small[0:16, 24:31], small[0:16, 8:15], AF.Exp,
                         bias=small[0:16, 17:18], scale=2.0))

        def wnorm(e):
            e.reduce_sum(small[0:16, 18:19], small[0:16, 24:31], axis=AX.X)
            e.drain()
            e.reciprocal(small[0:16, 19:20], small[0:16, 18:19])
            e.drain()
            e.tensor_scalar(small[0:16, 32:39], small[0:16, 24:31], small[0:16, 19:20], None, ALU.mult)

        sec(blk, "vector", [B_small], [B_small], wnorm)

    with _Block() as blk, \
         nc.psum_tensor(uname("pb_ps"), [128, 8], F32) as pb_ps, \
         nc.psum_tensor(uname("pw_ps"), [7, 16], F32) as pw_ps, \
         nc.psum_tensor(uname("pm_ps"), [6, 16], F32) as pm_ps:
        B_pb = Buf(); B_pw = Buf(); B_pm = Buf()

        sec(blk, "tensor", [B_small, B_const], [B_pb, B_pw], lambda e: (
            e.matmul(pb_ps[:, 0:4], ones_sb[0:1, :], small[0:1, 80:84], start=True, stop=True),
            e.transpose(pw_ps[:], small[0:16, 32:39], identf[0:16, 0:16])))

        sec(blk, "vector", [B_pb, B_pw], [B_bc, B_wT], lambda e: (
            e.tensor_copy(bc_sb[:, 0:4], pb_ps[:, 0:4]),
            e.tensor_copy(wT_sb[:], pw_ps[:])))

        for ei in range(6):
            sec(blk, "tensor", [B_small, B_wT], [B_pm],
                lambda e, ei=ei: e.matmul(pm_ps[:], small[0:7, 40 + ei * 6:46 + ei * 6],
                                          wT_sb[:], start=True, stop=True))
            mh3 = mh_sb.rearrange("d (h e) -> d h e", e=6)
            sec(blk, "vector", [B_pm], [B_mh],
                lambda e, ei=ei: e.tensor_copy(mh3[:, :, ei], pm_ps[:]))

    def emit_ln(blk, xin_ap, mvt, B_x, B_mvt, out_ap, B_out, lnt_half=1024):
        def stats(e):
            e.bn_stats(mvt[:, 0:6], xin_ap[:, 0:512])
            e.bn_stats(mvt[:, 6:12], xin_ap[:, 512:1024])
            e.drain()
            e.bn_aggr(mvt[:, 12:14], mvt[:, 0:12])
            e.drain()

        sec(blk, "vector", [B_x], [B_mvt], stats)

        def rstd(e):
            e.activation(mvt[:, 14:15], mvt[:, 13:14], AF.Ln, bias=1e-5, scale=1.0)
            e.drain()
            e.activation(mvt[:, 15:16], mvt[:, 14:15], AF.Exp, bias=0.0, scale=-0.5)

        sec(blk, "scalar", [B_mvt], [B_mvt], rstd)

        def norm(e):
            e.scalar_tensor_tensor(mvt[:, 14:15], mvt[:, 12:13], -1.0, mvt[:, 15:16],
                                   ALU.mult, ALU.mult)
            e.drain()
            e.tensor_scalar(out_ap, xin_ap, mvt[:, 15:16], mvt[:, 14:15], ALU.mult, ALU.add)

        sec(blk, "vector", [B_x, B_mvt], [B_mvt, B_out], norm)

    def seq_attention(s):
        fn0 = fence()
        esq = contextlib.ExitStack()
        q_sb = esq.enter_context(nc.sbuf_tensor(uname("q_sb"), [128, 16384], BF16))
        k_sb = esq.enter_context(nc.sbuf_tensor(uname("k_sb"), [128, 16384], BF16))
        v_sb = esq.enter_context(nc.sbuf_tensor(uname("v_sb"), [128, 16384], BF16))
        B_q = Buf(fn0); B_k = Buf(fn0); B_v = Buf(fn0)
        B_tq = Buf(fn0); B_tp = Buf(fn0); B_tp2 = Buf(fn0); B_hexc = Buf(fn0)

        with esq:
            esa = contextlib.ExitStack()
            fn = fence()
            ln1f = esa.enter_context(nc.sbuf_tensor(uname("ln1f"), [128, 16384], BF16))
            xt = [esa.enter_context(nc.sbuf_tensor(uname(f"xt{i}"), [128, 1024], F32)) for i in range(2)]
            lnt = [esa.enter_context(nc.sbuf_tensor(uname(f"lnt{i}"), [128, 1024], BF16)) for i in range(2)]
            mvt = esa.enter_context(nc.sbuf_tensor(uname("mvt"), [128, 16], F32))
            pqf = [esa.enter_context(nc.sbuf_tensor(uname(f"pqf{i}"), [128, 1024], F32)) for i in range(2)]
            pqb = [esa.enter_context(nc.sbuf_tensor(uname(f"pqb{i}"), [128, 1024], BF16)) for i in range(2)]
            B_ln1f = [Buf(fn) for _ in range(4)]
            B_xt = [Buf(fn), Buf(fn)]
            B_lnt = [Buf(fn), Buf(fn)]
            B_mvt = Buf(fn)
            B_pqf = [Buf(fn), Buf(fn)]
            B_pqb = [Buf(fn), Buf(fn)]
            ln4 = ln1f.rearrange("p (t j tt) -> p t j tt", t=16, j=8)

            with esa:
              if _on("a1"):
                with _Block() as blk, \
                     nc.psum_tensor(uname("lnp0"), [128, 1024], BF16) as lnp0, \
                     nc.psum_tensor(uname("lnp1"), [128, 1024], BF16) as lnp1:
                    lnp = [lnp0, lnp1]
                    B_lnp = [Buf(fn), Buf(fn)]
                    ka1 = os.environ.get("KA1", "dlte")
                    for t in range(int(os.environ.get("KNTT", NTT))):
                        tb = t % 2
                        if "d" in ka1:
                            dma_load(blk, "gpsimd", dx, xt[tb][:],
                                     x_d[s * T + t * 128: s * T + (t + 1) * 128, :], B_xt[tb],
                                     key=f"xt{tb}")
                        if "l" in ka1:
                            emit_ln(blk, xt[tb][:], mvt, B_xt[tb], B_mvt, lnt[tb][:], B_lnt[tb])

                        def tps(e, tb=tb):
                            for j in range(8):
                                e.transpose(lnp[tb][:, j * 128:(j + 1) * 128],
                                            lnt[tb][:, j * 128:(j + 1) * 128], identb[:])

                        if "t" in ka1:
                            sec(blk, "tensor", [B_lnt[tb], B_const], [B_lnp[tb]], tps)
                        if "e" in ka1:
                            sec(blk, "vector", [B_lnp[tb]], [B_ln1f[t // 4]],
                                lambda e, t=t, tb=tb: e.tensor_copy(
                                    ln1f[:, t * 1024:(t + 1) * 1024], lnp[tb][:]))

              if _on("a2p"):
                with _Block() as blk, \
                     nc.psum_tensor(uname("bh_ps"), [6, 512], F32) as bh_ps:
                    B_bh = Buf(fence())
                    for which in range(2):
                        wbf = hexw_bf if which == 0 else q6w_bf
                        Bw = B_hexw if which == 0 else B_q6w
                        for tc in range(4):
                            def bmm(e, wbf=wbf, tc=tc):
                                for kc in range(8):
                                    e.matmul(bh_ps[:], wbf[:, kc * 6:kc * 6 + 6],
                                             ln4[:, tc * 4:(tc + 1) * 4, kc, :],
                                             start=(kc == 0), stop=(kc == 7))

                            sec(blk, "tensor", [B_ln1f[tc], Bw], [B_bh], bmm)
                            if which == 0:
                                sec(blk, "scalar", [B_bh], [B_tq],
                                    lambda e, tc=tc: e.activation(
                                        tq_sb[0:6, tc * 512:(tc + 1) * 512], bh_ps[:],
                                        AF.Tanh, bias=0.0, scale=1.0))
                            else:
                                sec(blk, "scalar", [B_bh], [B_hexc],
                                    lambda e, tc=tc: e.activation(
                                        hexc_sb[:, tc * 512:(tc + 1) * 512], bh_ps[:],
                                        AF.Tanh, bias=0.0, scale=1.0))

                    sec(blk, "vector", [B_bc, B_tq], [B_tp, B_tp2], lambda e: (
                        e.tensor_scalar(tp_sb[0:6, :], tq_sb[0:6, :], LAMH, None, ALU.mult),
                        e.tensor_scalar(tp2_sb[0:6, :], tq_sb[0:6, :], LAMH, None, ALU.mult)))
                    merged_tq = dict(B_hexc.w)
                    for tk in (B_tq.w, B_tq.r):
                        for ss, vv in tk.items():
                            if merged_tq.get(ss, 0) < vv:
                                merged_tq[ss] = vv
                    _smtq = slot_sem("tq")
                    _smtq.n += 16
                    B_tq.note_w(_smtq, _smtq.n)
                    B_hexc.note_r(_smtq, _smtq.n)

                    def tq_dma(e, merged_tq=merged_tq, _smtq=_smtq):
                        _waits(e, gps, merged_tq)
                        e.dma_start(tq_sb[6:12, :], hexc_sb[:]).then_inc(_smtq.h, 16)

                    _defer("gpsimd", tq_dma)

              if _on("a3"):
                with _Block() as blk, \
                     nc.psum_tensor(uname("qk_ps0"), [128, 512], F32) as qk_ps0, \
                     nc.psum_tensor(uname("qk_ps1"), [128, 512], F32) as qk_ps1:
                    qk_ps = [qk_ps0, qk_ps1]
                    fnq = fence()
                    B_qk = [Buf(fnq), Buf(fnq)]
                    for m in range(16):
                        tb = m % 2
                        dma_load(blk, "sync", dw, pqf[tb][:], qkvw_d[m], B_pqf[tb], key=f"pqf{tb}")
                        sec(blk, "vector", [B_pqf[tb]], [B_pqb[tb]],
                            lambda e, tb=tb: e.tensor_copy(pqb[tb][:], pqf[tb][:]))
                        dst = q_sb if m < 8 else k_sb
                        dstB = B_q if m < 8 else B_k
                        c = m % 8
                        for tc in range(4):
                            pp = tc % 2

                            def qmm(e, tb=tb, tc=tc, pp=pp):
                                for kc in range(8):
                                    e.matmul(qk_ps[pp][:], pqb[tb][:, kc * 128:(kc + 1) * 128],
                                             ln4[:, tc * 4:(tc + 1) * 4, kc, :],
                                             start=(kc == 0), stop=(kc == 7))

                            sec(blk, "tensor", [B_pqb[tb], B_ln1f[tc]], [B_qk[pp]], qmm)
                            sec(blk, "scalar", [B_qk[pp]], [dstB],
                                lambda e, pp=pp, dst=dst, c=c, tc=tc: e.activation(
                                    dst[:, c * 2048 + tc * 512: c * 2048 + (tc + 1) * 512],
                                    qk_ps[pp][:], AF.Copy, bias=0.0, scale=1.0))

                    for vm in range(8):
                        tb = vm % 2
                        dma_load(blk, "sync", dw, pqf[tb][:], qkvw_d[16 + vm], B_pqf[tb], key=f"pqf{tb}")
                        sec(blk, "vector", [B_pqf[tb]], [B_pqb[tb]],
                            lambda e, tb=tb: e.tensor_copy(pqb[tb][:], pqf[tb][:]))
                        for kt in range(NTT):
                            pp = kt % 2

                            def vmm(e, tb=tb, kt=kt, pp=pp):
                                for kc in range(8):
                                    e.matmul(qk_ps[pp][:, 0:128],
                                             ln4[:, kt, kc, :],
                                             pqb[tb][:, kc * 128:(kc + 1) * 128],
                                             start=(kc == 0), stop=(kc == 7))

                            sec(blk, "tensor", [B_pqb[tb], B_ln1f[kt // 4]], [B_qk[pp]], vmm)
                            eng = "scalar" if kt % 2 == 0 else "vector"
                            dstap = v_sb[:, kt * 1024 + vm * 128: kt * 1024 + (vm + 1) * 128]
                            if eng == "scalar":
                                sec(blk, eng, [B_qk[pp]], [B_v],
                                    lambda e, pp=pp, dstap=dstap: e.activation(
                                        dstap, qk_ps[pp][:, 0:128], AF.Copy, bias=0.0, scale=1.0))
                            else:
                                sec(blk, eng, [B_qk[pp]], [B_v],
                                    lambda e, pp=pp, dstap=dstap: e.tensor_copy(
                                        dstap, qk_ps[pp][:, 0:128]))

            esc = contextlib.ExitStack()
            fn = fence()
            attn_sb = esc.enter_context(nc.sbuf_tensor(uname("attn_sb"), [128, 16384], BF16))
            esc2 = contextlib.ExitStack()
            ah_tmp = esc2.enter_context(nc.sbuf_tensor(uname("ah_tmp"), [6, 2048], BF16))
            S_sb = [esc2.enter_context(nc.sbuf_tensor(uname(f"S_sb{i}"), [128, 2048], F32)) for i in range(2)]
            th_sb = [esc2.enter_context(nc.sbuf_tensor(uname(f"th{i}"), [128, 512], F32)) for i in range(2)]
            Pn_sb = [esc2.enter_context(nc.sbuf_tensor(uname(f"Pn_sb{i}"), [128, 2048], BF16)) for i in range(2)]
            PT_sb = [esc2.enter_context(nc.sbuf_tensor(uname(f"PT_sb{i}"), [128, 2048], BF16)) for i in range(2)]
            rs_sb = [esc2.enter_context(nc.sbuf_tensor(uname(f"rs_sb{i}"), [128, 8], F32)) for i in range(2)]
            B_attn = Buf(fn); B_ahtmp = Buf(fn)
            B_th = [Buf(fn), Buf(fn)]
            B_S = [Buf(fn), Buf(fn)]
            B_Pn = [Buf(fn), Buf(fn)]
            B_PT = [Buf(fn), Buf(fn)]
            B_rs = [Buf(fn), Buf(fn)]

            with esc:
              if _on("attn"):
                with _Block() as blk, \
                     nc.psum_tensor(uname("raw0"), [128, 512], F32) as raw0, \
                     nc.psum_tensor(uname("raw1"), [128, 512], F32) as raw1, \
                     nc.psum_tensor(uname("bb0"), [128, 512], F32) as bb0, \
                     nc.psum_tensor(uname("bb1"), [128, 512], F32) as bb1, \
                     nc.psum_tensor(uname("ptb0"), [128, 1024], BF16) as ptb0, \
                     nc.psum_tensor(uname("ptb1"), [128, 1024], BF16) as ptb1, \
                     nc.psum_tensor(uname("pv_ps0"), [128, 512], F32) as pv_ps0, \
                     nc.psum_tensor(uname("pv_ps1"), [128, 512], F32) as pv_ps1:
                    raw = [raw0, raw1]
                    bb = [bb0, bb1]
                    ptb = [ptb0, ptb1]
                    B_raw = [Buf(fn), Buf(fn)]
                    B_bb = [Buf(fn), Buf(fn)]
                    B_ptb = [Buf(fn), Buf(fn)]
                    pv_ps = [pv_ps0, pv_ps1]
                    B_pv = [Buf(fn), Buf(fn)]
                    ah_ps = raw0[0:6, :]
                    B_ah = B_raw[0]
                    mh3 = mh_sb.rearrange("d (h e) -> d h e", e=6)

                    pend = []

                    _pdepth = int(os.environ.get("KPD", "2"))

                    def flush_pend(all_=True):
                        while pend and (all_ or len(pend) >= _pdepth):
                            pend.pop(0)()

                    def ah_prep(h, tpt, B_tpt):
                        for tc in range(4):
                            sec(blk, "tensor", [B_hexc, B_mh], [B_ah],
                                lambda e, h=h, tc=tc: e.matmul(
                                    ah_ps[:], mh3[:, h, :],
                                    hexc_sb[:, tc * 512:(tc + 1) * 512], start=True, stop=True))
                            sec(blk, "vector", [B_ah, B_bc], [B_ahtmp],
                                lambda e, tc=tc: e.tensor_scalar(
                                    ah_tmp[:, tc * 512:(tc + 1) * 512], ah_ps[:],
                                    CSC, None, ALU.mult))

                        merged_tp = dict(B_ahtmp.w)
                        for tk in (B_tpt.w, B_tpt.r):
                            for ss, vv in tk.items():
                                if merged_tp.get(ss, 0) < vv:
                                    merged_tp[ss] = vv
                        _smtp = slot_sem("tp" if tpt is tp_sb else "tp2")
                        _smtp.n += 16
                        B_tpt.note_w(_smtp, _smtp.n)
                        B_ahtmp.note_r(_smtp, _smtp.n)

                        def tp_dma(e, merged_tp=merged_tp, tpt=tpt, _smtp=_smtp):
                            _waits(e, gps, merged_tp)
                            e.dma_start(tpt[6:12, :], ah_tmp[:]).then_inc(_smtp.h, 16)

                        _defer("gpsimd", tp_dma)

                    def chain(h, qi):
                            row = (qi + 1) * 128
                            nkc = (row + 511) // 512
                            qo = qi * 128
                            par = h % 2
                            hb = par * 64
                            hc = h // 2
                            tpt = tp_sb if par == 0 else tp2_sb
                            B_tpt = B_tp if par == 0 else B_tp2
                            pni = par
                            S_c = S_sb[par]; Pn_c = Pn_sb[pni]
                            PT_c = PT_sb[par]; rs_c = rs_sb[par]
                            B_S_c = B_S[par]; B_Pn_c = B_Pn[pni]
                            B_PT_c = B_PT[par]; B_rs_c = B_rs[par]

                            for kc in range(nkc):
                                klen = min(512, row - kc * 512)
                                pp = kc % 2

                                diag = (kc == nkc - 1)

                                def qkmm(e, h=h, kc=kc, klen=klen, pp=pp, hb=hb, hc=hc,
                                         qo=qo, diag=diag, tpt=tpt):
                                    e.matmul(raw[pp][:, 0:klen],
                                             q_sb[hb:hb + 64, hc * 2048 + qo: hc * 2048 + qo + 128],
                                             k_sb[hb:hb + 64, hc * 2048 + kc * 512: hc * 2048 + kc * 512 + klen],
                                             start=True, stop=True)
                                    e.matmul(bb[pp][:, 0:klen],
                                             tpt[:, qo:qo + 128],
                                             tq_sb[:, kc * 512:kc * 512 + klen],
                                             start=True, stop=not diag)
                                    if diag:
                                        e.matmul(bb[pp][:, klen - 128:klen], identb[:], maskb[:],
                                                 start=False, stop=True)

                                sec(blk, "tensor", [B_q, B_k, B_tpt, B_tq, B_const],
                                    [B_raw[pp], B_bb[pp]], qkmm)
                                sec(blk, "scalar", [B_raw[pp], B_bc], [B_th[pp]],
                                    lambda e, kc=kc, klen=klen, pp=pp: e.activation(
                                        th_sb[pp][:, 0:klen], raw[pp][:, 0:klen],
                                        AF.Tanh, bias=0.0, scale=IVBETA))

                                def sttm(e, kc=kc, klen=klen, pp=pp, S_c=S_c, rs_c=rs_c,
                                         last=(kc == nkc - 1), nkc=nkc):
                                    e.scalar_tensor_tensor(
                                        S_c[:, kc * 512:kc * 512 + klen],
                                        th_sb[pp][:, 0:klen], ALPHA, bb[pp][:, 0:klen],
                                        ALU.mult, ALU.add)
                                    e.drain()
                                    if nkc == 1:
                                        e.tensor_reduce(rs_c[:, 0:1], S_c[:, 0:klen],
                                                        axis=AX.X, op=ALU.max, negate=True)
                                        return
                                    e.tensor_reduce(rs_c[:, 4 + kc:5 + kc],
                                                    S_c[:, kc * 512:kc * 512 + klen],
                                                    axis=AX.X, op=ALU.max)
                                    if last:
                                        e.drain()
                                        e.tensor_reduce(rs_c[:, 0:1], rs_c[:, 4:4 + nkc],
                                                        axis=AX.X, op=ALU.max, negate=True)

                                sec(blk, "vector", [B_th[pp], B_bb[pp], B_bc],
                                    [B_S_c, B_rs_c], sttm)

                            def part2(row=row, S_c=S_c, Pn_c=Pn_c, rs_c=rs_c,
                                      B_S_c=B_S_c, B_Pn_c=B_Pn_c, B_rs_c=B_rs_c,
                                      h=h, qi=qi, hb=hb, hc=hc, PT_c=PT_c, B_PT_c=B_PT_c):
                              if os.environ.get("KPIPE", "1") == "1":
                                flush_pend(all_=False)
                              sec(blk, "scalar", [B_S_c, B_rs_c], [B_Pn_c, B_rs_c],
                                  lambda e: e.activation(
                                      Pn_c[:, 0:row], S_c[:, 0:row], AF.Exp,
                                      bias=rs_c[:, 0:1], scale=1.0, accum_out=rs_c[:, 1:2]))

                              def pnorm(e):
                                  e.reciprocal(rs_c[:, 2:3], rs_c[:, 1:2])
                                  e.drain()
                                  e.tensor_scalar(Pn_c[:, 0:row], Pn_c[:, 0:row],
                                                  rs_c[:, 2:3], None, ALU.mult)

                              sec(blk, "vector", [B_Pn_c, B_rs_c], [B_Pn_c, B_rs_c], pnorm)

                            def _mk_tail(h, qi, hb, hc, Pn_c, PT_c, B_Pn_c, B_PT_c):
                              def tail():
                                nbank = (qi + 8) // 8
                                for b in range(nbank):
                                    ktlo = b * 8
                                    kthi = min(qi + 1, ktlo + 8)

                                    def ptmm(e, b=b, ktlo=ktlo, kthi=kthi):
                                        for kt in range(ktlo, kthi):
                                            e.transpose(ptb[b][:, (kt - ktlo) * 128:(kt - ktlo + 1) * 128],
                                                        Pn_c[:, kt * 128:(kt + 1) * 128], identb[:])

                                    sec(blk, "tensor", [B_Pn_c, B_const], [B_ptb[b]], ptmm)
                                    sec(blk, "scalar", [B_ptb[b]], [B_PT_c],
                                        lambda e, b=b, ktlo=ktlo, kthi=kthi: e.activation(
                                            PT_c[:, ktlo * 128: kthi * 128],
                                            ptb[b][:, 0:(kthi - ktlo) * 128],
                                            AF.Copy, bias=0.0, scale=1.0))

                                pvp = pv_ps[h % 2]

                                def pvmm(e, pvp=pvp):
                                    for kt in range(qi + 1):
                                        e.matmul(pvp[0:64, 0:128],
                                                 v_sb[:, kt * 1024 + h * 64: kt * 1024 + (h + 1) * 64],
                                                 PT_c[:, kt * 128:(kt + 1) * 128],
                                                 start=(kt == 0), stop=(kt == qi))

                                sec(blk, "tensor", [B_PT_c, B_v], [B_pv[h % 2]], pvmm)

                                dstap = attn_sb[hb:hb + 64, hc * 2048 + qi * 128: hc * 2048 + (qi + 1) * 128]
                                if h % 2 == 0:
                                    sec(blk, "scalar", [B_pv[h % 2]], [B_attn],
                                        lambda e, pvp=pvp: e.activation(
                                            dstap, pvp[0:64, 0:128], AF.Copy, bias=0.0, scale=1.0))
                                else:
                                    sec(blk, "vector", [B_pv[h % 2]], [B_attn],
                                        lambda e, pvp=pvp: e.tensor_copy(dstap, pvp[0:64, 0:128]))
                              return tail

                            return part2, _mk_tail

                    def run_pair_qi(h, qi):
                        part2, mk = chain_parts[(h, qi)]
                        part2()
                        tail = mk(h, qi, (h % 2) * 64, h // 2,
                                  Pn_sb[h % 2], PT_sb[h % 2],
                                  B_Pn[h % 2], B_PT[h % 2])
                        if os.environ.get("KPIPE", "1") == "1":
                            pend.append(tail)
                        else:
                            tail()

                    chain_parts = {}
                    for hp in range(H // 2):
                        ah_prep(2 * hp, tp_sb, B_tp)
                        ah_prep(2 * hp + 1, tp2_sb, B_tp2)
                        for qi in range(NTT):
                            chain_parts[(2 * hp, qi)] = chain(2 * hp, qi)
                            chain_parts[(2 * hp + 1, qi)] = chain(2 * hp + 1, qi)
                            run_pair_qi(2 * hp, qi)
                            run_pair_qi(2 * hp + 1, qi)

                    flush_pend()

                esc2.close()

                if not _on("oproj"):
                    return
                eso = contextlib.ExitStack()
                fno = fence()
                xr = [eso.enter_context(nc.sbuf_tensor(uname(f"xr{i}"), [128, 1024], F32)) for i in range(2)]
                x1t = [eso.enter_context(nc.sbuf_tensor(uname(f"x1t{i}"), [128, 1024], F32)) for i in range(2)]
                B_xr = [Buf(fno), Buf(fno)]
                B_x1t = [Buf(fno), Buf(fno)]
                with eso:
                    with _Block() as blk, \
                         nc.psum_tensor(uname("op0"), [128, 512], F32) as op0, \
                         nc.psum_tensor(uname("op1"), [128, 512], F32) as op1:
                        op_ps = [op0, op1]
                        B_op = [Buf(fno), Buf(fno)]
                        for qi in range(NTT):
                            tb = qi % 2
                            dma_load(blk, "gpsimd", dx, xr[tb][:],
                                     x_d[s * T + qi * 128: s * T + (qi + 1) * 128, :], B_xr[tb],
                                     key=f"xr{tb}")
                            for nf in range(2):
                                def opmm(e, qi=qi, nf=nf):
                                    for c in range(8):
                                        e.matmul(op_ps[nf][:],
                                                 attn_sb[:, c * 2048 + qi * 128: c * 2048 + (qi + 1) * 128],
                                                 outw_bf[:, c * 1024 + nf * 512: c * 1024 + (nf + 1) * 512],
                                                 start=(c == 0), stop=(c == 7))

                                sec(blk, "tensor", [B_attn, B_outw], [B_op[nf]], opmm)
                                sec(blk, "vector", [B_op[nf], B_xr[tb]], [B_x1t[tb]],
                                    lambda e, qi=qi, nf=nf, tb=tb: e.tensor_tensor(
                                        x1t[tb][:, nf * 512:(nf + 1) * 512], op_ps[nf][:],
                                        xr[tb][:, nf * 512:(nf + 1) * 512], ALU.add))

                            dma_store(blk, "gpsimd", dx,
                                      x1_d[s * T + qi * 128: s * T + (qi + 1) * 128, :],
                                      x1t[tb][:], B_x1t[tb], key=f"x1t{tb}")

    def stream_panel4(blk, wd_m, phf, phb, B_phf, B_phb):
        src = wd_m.rearrange("p k j -> p (k j)")
        for qq in range(4):
            qb = qq % 2
            hh = qq // 2
            sl = qq % 2
            dma_load(blk, "sync", dw, phf[qb][:],
                     src[:, qq * 1024:(qq + 1) * 1024], B_phf[qb], key=f"phf{qb}")
            sec(blk, "vector", [B_phf[qb]], [B_phb[hh]],
                lambda e, qb=qb, hh=hh, sl=sl: e.tensor_copy(
                    phb[hh][:, sl * 1024:(sl + 1) * 1024], phf[qb][:]))

    def seq_ffn(s):
        for ft in range(NFT):
            base = s * T + ft * FTOK
            est = contextlib.ExitStack()
            fn = fence()
            h0 = est.enter_context(nc.sbuf_tensor(uname("h0"), [128, 32768], BF16))
            phf = [est.enter_context(nc.sbuf_tensor(uname(f"phf{i}"), [128, 1024], F32)) for i in range(2)]
            phb = [est.enter_context(nc.sbuf_tensor(uname(f"phb{i}"), [128, 2048], BF16)) for i in range(2)]
            B_h0 = Buf(fn)
            B_phf = [Buf(fn), Buf(fn)]
            B_phb = [Buf(fn), Buf(fn)]
            h03 = h0.rearrange("p (c t) -> p c t", c=32)

            with est:
                es1 = contextlib.ExitStack()
                ln2f = es1.enter_context(nc.sbuf_tensor(uname("ln2f"), [128, 8192], BF16))
                B_ln2f = [Buf(fn) for _ in range(2)]
                ln24 = ln2f.rearrange("p (t j tt) -> p t j tt", t=8, j=8)
                with es1:
                    es2 = contextlib.ExitStack()
                    fn2 = fence()
                    x1f = es2.enter_context(nc.sbuf_tensor(uname("x1f"), [128, 8192], F32))
                    lnt2 = [es2.enter_context(nc.sbuf_tensor(uname(f"lnt2{i}"), [128, 1024], BF16)) for i in range(2)]
                    mvt2 = es2.enter_context(nc.sbuf_tensor(uname("mvt2"), [128, 16], F32))
                    B_x1f = Buf(fn2)
                    B_lnt2 = [Buf(fn2), Buf(fn2)]
                    B_mvt2 = Buf(fn2)
                    with es2:
                      if _on("fln"):
                        with _Block() as blk, \
                             nc.psum_tensor(uname("lnq0"), [128, 1024], BF16) as lnq0, \
                             nc.psum_tensor(uname("lnq1"), [128, 1024], BF16) as lnq1:
                            lnq = [lnq0, lnq1]
                            B_lnq = [Buf(fn2), Buf(fn2)]
                            dma_load(blk, "gpsimd", dx,
                                     x1f[:].rearrange("p (a d) -> p a d", a=8),
                                     x1_d[base: base + FTOK, :].rearrange("(a p) d -> p a d", p=128),
                                     B_x1f, key="x1f")
                            for st in range(8):
                                tb = st % 2
                                emit_ln(blk, x1f[:, st * 1024:(st + 1) * 1024], mvt2,
                                        B_x1f, B_mvt2, lnt2[tb][:], B_lnt2[tb])

                                def tps2(e, tb=tb):
                                    for j in range(8):
                                        e.transpose(lnq[tb][:, j * 128:(j + 1) * 128],
                                                    lnt2[tb][:, j * 128:(j + 1) * 128], identb[:])

                                sec(blk, "tensor", [B_lnt2[tb], B_const], [B_lnq[tb]], tps2)
                                sec(blk, "vector", [B_lnq[tb]], [B_ln2f[st // 4]],
                                    lambda e, st=st, tb=tb: e.tensor_copy(
                                        ln2f[:, st * 1024:(st + 1) * 1024], lnq[tb][:]))

                    es3 = contextlib.ExitStack()
                    fn3 = fence()
                    pwf = [es3.enter_context(nc.sbuf_tensor(uname(f"pwf{i}"), [128, 1024], F32)) for i in range(2)]
                    pwb = [es3.enter_context(nc.sbuf_tensor(uname(f"pwb{i}"), [128, 1024], BF16)) for i in range(2)]
                    sil = [es3.enter_context(nc.sbuf_tensor(uname(f"sil{i}"), [128, 512], BF16)) for i in range(2)]
                    B_pwf = [Buf(fn3), Buf(fn3)]
                    B_pwb = [Buf(fn3), Buf(fn3)]
                    B_sil = [Buf(fn3), Buf(fn3)]
                    with es3:
                      if _on("fw12"):
                        with _Block() as blk, \
                             nc.psum_tensor(uname("wa0"), [128, 512], F32) as wa0, \
                             nc.psum_tensor(uname("wa1"), [128, 512], F32) as wa1, \
                             nc.psum_tensor(uname("wb0"), [128, 512], F32) as wb0, \
                             nc.psum_tensor(uname("wb1"), [128, 512], F32) as wb1:
                            wa = [wa0, wa1]
                            wb = [wb0, wb1]
                            B_wa = [Buf(fn3), Buf(fn3)]
                            B_wb = [Buf(fn3), Buf(fn3)]
                            for m in range(32):
                                dma_load(blk, "sync", dw, pwf[0][:], w1_d[m], B_pwf[0], key="pwf0")
                                sec(blk, "vector", [B_pwf[0]], [B_pwb[0]],
                                    lambda e: e.tensor_copy(pwb[0][:], pwf[0][:]))
                                dma_load(blk, "sync", dw, pwf[1][:], w2_d[m], B_pwf[1], key="pwf1")
                                sec(blk, "vector", [B_pwf[1]], [B_pwb[1]],
                                    lambda e: e.tensor_copy(pwb[1][:], pwf[1][:]))
                                for tc in range(2):
                                    def w1mm(e, tc=tc):
                                        for kc in range(8):
                                            e.matmul(wa[tc][:], pwb[0][:, kc * 128:(kc + 1) * 128],
                                                     ln24[:, tc * 4:(tc + 1) * 4, kc, :],
                                                     start=(kc == 0), stop=(kc == 7))

                                    def w2mm(e, tc=tc):
                                        for kc in range(8):
                                            e.matmul(wb[tc][:], pwb[1][:, kc * 128:(kc + 1) * 128],
                                                     ln24[:, tc * 4:(tc + 1) * 4, kc, :],
                                                     start=(kc == 0), stop=(kc == 7))

                                    sec(blk, "tensor", [B_pwb[0], B_ln2f[tc]], [B_wa[tc]], w1mm)
                                    sec(blk, "tensor", [B_pwb[1], B_ln2f[tc]], [B_wb[tc]], w2mm)
                                    sec(blk, "scalar", [B_wa[tc], B_sc], [B_sil[tc]],
                                        lambda e, m=m, tc=tc: e.activation(
                                            sil[tc][:], wa[tc][:], AF.Silu,
                                            bias=0.0, scale=sc_sb[:, m:m + 1]))
                                    sec(blk, "vector", [B_wb[tc], B_sil[tc], B_sc], [B_h0],
                                        lambda e, m=m, tc=tc: e.scalar_tensor_tensor(
                                            h0[:, m * 1024 + tc * 512: m * 1024 + (tc + 1) * 512],
                                            wb[tc][:], sc_sb[:, 32 + m:33 + m], sil[tc][:],
                                            ALU.mult, ALU.mult))

                es4 = contextlib.ExitStack()
                fn4 = fence()
                g1 = es4.enter_context(nc.sbuf_tensor(uname("g1"), [128, 32768], BF16))
                B_g1 = Buf(fn4)
                g13 = g1.rearrange("p (c t) -> p c t", c=32)
                with es4:
                    for stage in range(2 if _on("fh") else 0):
                        wd = h1_d if stage == 0 else h2_d
                        srcB = B_h0 if stage == 0 else B_g1
                        src3 = h03 if stage == 0 else g13
                        dst = g1 if stage == 0 else h0
                        dstB = B_g1 if stage == 0 else B_h0
                        scol = 64 if stage == 0 else 96
                        with _Block() as blk, \
                             nc.psum_tensor(uname("gp0"), [128, 512], F32) as gp0, \
                             nc.psum_tensor(uname("gp1"), [128, 512], F32) as gp1:
                            gp = [gp0, gp1]
                            fng = fence()
                            B_gp = [Buf(fng), Buf(fng)]
                            for m in range(32):
                                stream_panel4(blk, wd[m], phf, phb, B_phf, B_phb)
                                for tc in range(2):
                                    def hmm1(e, tc=tc, src3=src3):
                                        for kc in range(16):
                                            e.matmul(gp[tc][:], phb[0][:, kc * 128:(kc + 1) * 128],
                                                     src3[:, kc, tc * 512:(tc + 1) * 512],
                                                     start=(kc == 0), stop=False)

                                    def hmm2(e, tc=tc, src3=src3):
                                        for kc in range(16):
                                            e.matmul(gp[tc][:], phb[1][:, kc * 128:(kc + 1) * 128],
                                                     src3[:, 16 + kc, tc * 512:(tc + 1) * 512],
                                                     start=False, stop=(kc == 15))

                                    sec(blk, "tensor", [B_phb[0], srcB], [B_gp[tc]], hmm1)
                                    sec(blk, "tensor", [B_phb[1], srcB], [B_gp[tc]], hmm2)
                                    sec(blk, "scalar", [B_gp[tc], B_sc], [dstB],
                                        lambda e, m=m, tc=tc, dst=dst, scol=scol, gp=gp: e.activation(
                                            dst[:, m * 1024 + tc * 512: m * 1024 + (tc + 1) * 512],
                                            gp[tc][:], AF.Gelu, bias=0.0,
                                            scale=sc_sb[:, scol + m:scol + m + 1]))
                            _flush()

                es5 = contextlib.ExitStack()
                fn5 = fence()
                of = es5.enter_context(nc.sbuf_tensor(uname("of"), [128, 8192], F32))
                x1g = es5.enter_context(nc.sbuf_tensor(uname("x1g"), [128, 8192], F32))
                yt = [es5.enter_context(nc.sbuf_tensor(uname(f"yt{i}"), [128, 1024], F32)) for i in range(2)]
                ystg = es5.enter_context(nc.sbuf_tensor(uname("ystg"), [128, 1024], F32))
                B_of = Buf(fn5)
                B_x1g = Buf(fn5)
                B_yt = [Buf(fn5), Buf(fn5)]
                B_ystg = Buf(fn5)
                with es5:
                  if _on("fw3"):
                    with _Block() as blk, \
                         nc.psum_tensor(uname("o3p0"), [128, 512], F32) as o3p0, \
                         nc.psum_tensor(uname("o3p1"), [128, 512], F32) as o3p1:
                        o3p = [o3p0, o3p1]
                        B_o3 = [Buf(fn5), Buf(fn5)]
                        dma_load(blk, "gpsimd", dx,
                                 x1g[:].rearrange("p (a d) -> p a d", a=8),
                                 x1_d[base: base + FTOK, :].rearrange("(a p) d -> p a d", p=128),
                                 B_x1g, key="x1g")
                        for m in range(8):
                            stream_panel4(blk, w3_d[m], phf, phb, B_phf, B_phb)
                            for tc in range(2):
                                def w3mm1(e, tc=tc):
                                    for kc in range(16):
                                        e.matmul(o3p[tc][:], phb[0][:, kc * 128:(kc + 1) * 128],
                                                 h03[:, kc, tc * 512:(tc + 1) * 512],
                                                 start=(kc == 0), stop=False)

                                def w3mm2(e, tc=tc):
                                    for kc in range(16):
                                        e.matmul(o3p[tc][:], phb[1][:, kc * 128:(kc + 1) * 128],
                                                 h03[:, 16 + kc, tc * 512:(tc + 1) * 512],
                                                 start=False, stop=(kc == 15))

                                sec(blk, "tensor", [B_phb[0], B_h0], [B_o3[tc]], w3mm1)
                                sec(blk, "tensor", [B_phb[1], B_h0], [B_o3[tc]], w3mm2)
                                sec(blk, "vector", [B_o3[tc], B_sc], [B_of],
                                    lambda e, m=m, tc=tc: e.tensor_scalar(
                                        of[:, m * 1024 + tc * 512: m * 1024 + (tc + 1) * 512],
                                        o3p[tc][:], sc_sb[:, 128 + m:129 + m], None, ALU.mult))

                  if _on("fout"):
                    with _Block() as blk, \
                         nc.psum_tensor(uname("otp0"), [128, 512], F32) as otp0, \
                         nc.psum_tensor(uname("otp1"), [128, 512], F32) as otp1:
                        otp = [otp0, otp1]
                        fn6 = fence()
                        B_ot = [Buf(fn6), Buf(fn6)]
                        for st in range(8):
                            tb = st % 2
                            for bk in range(2):
                                def otmm(e, st=st, bk=bk):
                                    for cc in range(4):
                                        c = bk * 4 + cc
                                        e.transpose(otp[bk][:, cc * 128:(cc + 1) * 128],
                                                    of[:, c * 1024 + st * 128: c * 1024 + (st + 1) * 128],
                                                    identf[:])

                                sec(blk, "tensor", [B_of, B_const], [B_ot[bk]], otmm)
                                sec(blk, "vector", [B_ot[bk], B_x1g], [B_yt[tb]],
                                    lambda e, st=st, tb=tb, bk=bk: e.tensor_tensor(
                                        yt[tb][:, bk * 512:(bk + 1) * 512], otp[bk][:],
                                        x1g[:, st * 1024 + bk * 512: st * 1024 + bk * 512 + 512],
                                        ALU.add))

                            dma_store(blk, "gpsimd", dx,
                                      y_d[base + st * 128: base + (st + 1) * 128, :],
                                      yt[tb][:], B_yt[tb], key=f"yt{tb}")
                        _flush()

    nseq = int(os.environ.get("KSEQS", SEQS))
    with es:
        for s in range(nseq):
            if _on("seqattn"):
                seq_attention(s)
            group_boundary()
            if _on("seqffn"):
                seq_ffn(s)
            group_boundary()

        _fin = [(sm.h, sm.n) for sm in ALLSEMS[5:] if sm.n > 0]

        def final(e):
            for hh, vv in _fin:
                e.wait_ge(hh, vv)

        _defer("sync", final)
        group_boundary()

    return nc


def _stage_w(w):
    O, I = w.shape
    return np.ascontiguousarray(
        w.reshape(O // 128, 128, I // 128, 128).transpose(0, 3, 2, 1)).astype(np.float32)


def kernel(
    L0_attn_x, L0_attn_qkv_w, L0_attn_out_w, L0_attn_hex_w,
    L0_attn_hamming_lambda_logit, L0_attn_q6_w, L0_attn_transforms,
    L0_attn_transform_weights, L0_attn_scale_logit, L0_attn_sips_alpha,
    L0_attn_sips_beta, L0_ffn_w1_w, L0_ffn_w1_s, L0_ffn_w2_w, L0_ffn_w2_s,
    L0_ffn_h1_w, L0_ffn_h1_s, L0_ffn_h2_w, L0_ffn_h2_s, L0_ffn_w3_w, L0_ffn_w3_s,
):
    x = np.ascontiguousarray(np.asarray(L0_attn_x, dtype=np.float32))
    B = x.shape[0]
    n_cores = 8
    per = B // n_cores

    shared = {
        "qkvw": _stage_w(np.asarray(L0_attn_qkv_w, np.float32)),
        "outw": np.ascontiguousarray(
            np.asarray(L0_attn_out_w, np.float32).T.reshape(8, 128, 8, 128)),
        "w1": _stage_w(np.asarray(L0_ffn_w1_w, np.float32)),
        "w2": _stage_w(np.asarray(L0_ffn_w2_w, np.float32)),
        "h1": _stage_w(np.asarray(L0_ffn_h1_w, np.float32)),
        "h2": _stage_w(np.asarray(L0_ffn_h2_w, np.float32)),
        "w3": _stage_w(np.asarray(L0_ffn_w3_w, np.float32)),
        "s1": np.ascontiguousarray(np.asarray(L0_ffn_w1_s, np.float32).reshape(32, 128).T),
        "s2": np.ascontiguousarray(np.asarray(L0_ffn_w2_s, np.float32).reshape(32, 128).T),
        "sh1": np.ascontiguousarray(np.asarray(L0_ffn_h1_s, np.float32).reshape(32, 128).T),
        "sh2": np.ascontiguousarray(np.asarray(L0_ffn_h2_s, np.float32).reshape(32, 128).T),
        "s3": np.ascontiguousarray(np.asarray(L0_ffn_w3_s, np.float32).reshape(8, 128).T),
        "hexw": np.ascontiguousarray(np.asarray(L0_attn_hex_w, np.float32).T.reshape(8, 128, 6)),
        "q6w": np.ascontiguousarray(np.asarray(L0_attn_q6_w, np.float32).T.reshape(8, 128, 6)),
        "tf": np.ascontiguousarray(np.asarray(L0_attn_transforms, np.float32).transpose(0, 2, 1).reshape(7, 36)),
        "tw": np.ascontiguousarray(np.asarray(L0_attn_transform_weights, np.float32)),
        "lamlog": np.asarray(L0_attn_hamming_lambda_logit, np.float32).reshape(1, 1),
        "slog": np.asarray(L0_attn_scale_logit, np.float32).reshape(1, 1),
        "alpha": np.asarray(L0_attn_sips_alpha, np.float32).reshape(1, 1),
        "beta": np.asarray(L0_attn_sips_beta, np.float32).reshape(1, 1),
        "identf": np.eye(128, dtype=np.float32),
        "identb": np.eye(128).astype(ml_dtypes.bfloat16),
        "maskt": np.where(np.arange(128)[None, :] > np.arange(128)[:, None],
                          np.float32(NEG), np.float32(0.0)).astype(np.float32),
        "ones": np.ones((1, 128), np.float32),
    }

    nc = build_nc()
    in_maps = []
    for i in range(n_cores):
        m = dict(shared)
        m["x"] = np.ascontiguousarray(x[i * per:(i + 1) * per].reshape(per * T, D))
        in_maps.append(m)

    res = run_bass_kernel_spmd(nc, in_maps, core_ids=list(range(n_cores)))
    out = np.concatenate(
        [np.asarray(res.results[i]["y"]).reshape(per, T, D) for i in range(n_cores)], axis=0)
    return out.astype(np.float32)
